# version 04
# baseline (speedup 1.0000x reference)
from contextlib import ExitStack

import numpy as np

import concourse.bacc as bacc
import concourse.bass as bass
import concourse.mybir as mybir
from concourse.bass import IndirectOffsetOnAxis
from concourse.bass_utils import run_bass_kernel_spmd

F32 = mybir.dt.float32
BF16 = mybir.dt.bfloat16
I32 = mybir.dt.int32
I16 = mybir.dt.int16
AF = mybir.ActivationFunctionType
ALU = mybir.AluOpType
AX = mybir.AxisListType

T = 4096
D = 1024
NT = 32
ND = 8
NE = 16
DFF = 4096
CAP = 256
NCH = 4
EPS = 1e-5
SCALE = 1.0 / 32.0
BIS = 38
SINK = 6


class Sem:
    def __init__(self, ctx, nc, name):
        self.h = ctx.enter_context(nc.semaphore(name))
        self.v = 0

    def inc(self, instr, n=1):
        instr.then_inc(self.h, n)
        self.v += n
        return self.v


def build_nc(dbg=False, upto=5, ne_lim=NE, e_stage=9):
    nc = bacc.Bacc("TRN2")
    ctx = ExitStack()
    Et = ctx.enter_context

    x_h = nc.dram_tensor("x", [T, D], F32, kind="ExternalInput")
    wq_h = nc.dram_tensor("wq", [D, D], F32, kind="ExternalInput")
    wk_h = nc.dram_tensor("wk", [D, D], F32, kind="ExternalInput")
    wv_h = nc.dram_tensor("wv", [D, D], F32, kind="ExternalInput")
    gq_h = nc.dram_tensor("gq", [NE, D], F32, kind="ExternalInput")
    lt_h = nc.dram_tensor("ltemp", [1, 1], F32, kind="ExternalInput")
    w1_h = nc.dram_tensor("w1", [NE, D, DFF], F32, kind="ExternalInput")
    b1_h = nc.dram_tensor("b1", [NE, DFF], F32, kind="ExternalInput")
    w2_h = nc.dram_tensor("w2", [NE, DFF, D], F32, kind="ExternalInput")
    b2_h = nc.dram_tensor("b2", [NE, D], F32, kind="ExternalInput")
    out_h = nc.dram_tensor("out", [T, D], F32, kind="ExternalOutput")
    ikind = "ExternalOutput" if dbg else "Internal"
    x2_h = nc.dram_tensor("x2h", [T, D], BF16, kind=ikind)
    ln3_h = nc.dram_tensor("ln3h", [T, D], BF16, kind=ikind)
    acc_h = nc.dram_tensor("acch", [T, D], F32, kind=ikind)
    adram = nc.dram_tensor("adram", [16, 4096], F32, kind="Internal")
    if dbg:
        gdbg_h = nc.dram_tensor("gdbg", [NE, T], F32, kind="ExternalOutput")
        kdbg_h = nc.dram_tensor("kdbg", [128, ND * T], BF16, kind="ExternalOutput")
        vdbg_h = nc.dram_tensor("vdbg", [128, NT * D], BF16, kind="ExternalOutput")
        adbg_h = nc.dram_tensor("adbg", [128, 512], F32, kind="ExternalOutput")
        idbg_h = nc.dram_tensor("idbg", [128, 2 * NE], F32, kind="ExternalOutput")
        wdbg_h = nc.dram_tensor("wdbg", [128, 2 * NE], F32, kind="ExternalOutput")
        eidbg_h = nc.dram_tensor("eidbg", [128, 2048], BF16, kind="ExternalOutput")
        efdbg_h = nc.dram_tensor("efdbg", [128, 2048], BF16, kind="ExternalOutput")
        hdbg_h = nc.dram_tensor("hdbg", [128, 8192], BF16, kind="ExternalOutput")
        eodbg_h = nc.dram_tensor("eodbg", [128, 2048], F32, kind="ExternalOutput")
        w2dbg_h = nc.dram_tensor("w2dbg", [128, NT * D], BF16, kind="ExternalOutput")
        b2dbg_h = nc.dram_tensor("b2dbg", [33, 1024], BF16, kind="ExternalOutput")

    ar_a = Et(nc.sbuf_tensor("ar_a", [128, ND * T], BF16))
    ar_b = Et(nc.sbuf_tensor("ar_b", [128, NT * D], BF16))
    a3 = ar_a[:, :].rearrange("p (a b) -> p a b", b=T)
    b3 = ar_b[:, :].rearrange("p (a b) -> p a b", b=D)

    sbig = Et(nc.sbuf_tensor("sbig", [128, 8192], BF16))
    sbigE = sbig[:, 0:8192].rearrange("p (a b) -> p a b", b=256)

    sb_f32 = Et(nc.sbuf_tensor("sbf32", [128, 2048], F32))
    stg = Et(nc.sbuf_tensor("stg", [128, 3072], BF16))

    wq_s = Et(nc.sbuf_tensor("wq_s", [128, ND * D], BF16))
    wk_s = Et(nc.sbuf_tensor("wk_s", [128, ND * D], BF16))
    wv_s = Et(nc.sbuf_tensor("wv_s", [128, ND * D], BF16))
    wq3 = wq_s[:, :].rearrange("p (a b) -> p a b", b=D)
    wk3 = wk_s[:, :].rearrange("p (a b) -> p a b", b=D)
    wv3 = wv_s[:, :].rearrange("p (a b) -> p a b", b=D)
    gt = nc.alloc_sbuf_tensor_at("gt", [128, 4096], F32,
                                 offset=nc.lookup_mloc(wk_s).addr)
    em = nc.alloc_sbuf_tensor_at("em", [128, 4096], F32,
                                 offset=nc.lookup_mloc(wv_s).addr)

    gq_s = ar_a[0:16, 0:1024]
    gqT = Et(nc.sbuf_tensor("gqT", [128, 128], BF16))
    g_em = em[0:16, 0:4096]
    aemv = nc.alloc_sbuf_tensor_at("aemv", [16, 4096], F32,
                                   offset=nc.lookup_mloc(wq_s).addr)
    lt_s = Et(nc.sbuf_tensor("lt_s", [1, 1], F32))
    rt1 = Et(nc.sbuf_tensor("rt1", [1, 1], F32))
    rt16 = Et(nc.sbuf_tensor("rt16", [16, 1], F32))

    idt_b = Et(nc.sbuf_tensor("idt_b", [128, 128], BF16))
    idt_f = Et(nc.sbuf_tensor("idt_f", [128, 128], F32))
    iof = Et(nc.sbuf_tensor("iof", [128, 128], F32))
    iop = Et(nc.sbuf_tensor("iop", [128, 1], F32))

    ln1t = sbig[:, 7168:8192]
    mst = Et(nc.sbuf_tensor("mst", [128, 24], F32))
    hz = Et(nc.sbuf_tensor("hz", [1, 8], F32))
    epsb = Et(nc.sbuf_tensor("epsb", [128, 1], F32))

    pmod = Et(nc.sbuf_tensor("pmod", [128, 1], F32))
    pdiv = Et(nc.sbuf_tensor("pdiv", [128, 1], F32))
    ind_e = sb_f32[:, 1296:1312]
    carry_m = sb_f32[:, 1152:1280]
    lo_s = sb_f32[:, 1280:1281]
    hi_s = sb_f32[:, 1281:1282]
    mid_s = sb_f32[:, 1282:1283]
    cnt_p = sb_f32[:, 1283:1284]
    c16 = sb_f32[0:16, 1284:1285]
    cb_s = sb_f32[:, 1285:1286]
    i32v = nc.alloc_sbuf_tensor_at(
        "i32v", [128, 2], I32,
        offset=nc.lookup_mloc(sb_f32).addr + 1376 * 4)
    msk_s = i32v[:, 0:1]
    nmsk_s = i32v[:, 1:2]
    m16 = sb_f32[0:16, 1286:1287]
    nm16 = sb_f32[0:16, 1287:1288]
    rs16 = sb_f32[0:16, 1288:1289]
    seg32 = sb_f32[:, 1312:1344]
    vi16 = nc.alloc_sbuf_tensor_at("vi16", [128, 4096], I16,
                                   offset=nc.lookup_mloc(sbig).addr)
    tokid = vi16[:, 0:512]
    toff = sb_f32[:, 1289:1290]
    msi = vi16[:, 512:1024]
    a_bf = sbig[:, 3072:3584]
    dstA = vi16[:, 1024:1280]
    dstB = sbig[:, 3584:3840]
    idx_f = sb_f32[:, 1344:1376]
    gat_c = em[:, 320:352]
    idx_i = nc.alloc_sbuf_tensor_at(
        "idx_i", [128, 2 * NE], I32,
        offset=nc.lookup_mloc(wv_s).addr + 12288)

    ei_sm = stg[:, 1024:3072].rearrange("p (a b) -> p a b", b=D)
    eif_t = nc.alloc_sbuf_tensor_at(
        "eif_t", [128, 2048], BF16, offset=nc.lookup_mloc(wv_s).addr + 8192)
    stgE = eif_t[:, :].rearrange("p (a b) -> p a b", b=256)
    b1c = em[:, 256:320].rearrange("p (a b) -> p a b", b=32)
    b1raw = em[0:32, 0:256]
    b2sl = [wq_s[32:33, 0:1024], wq_s[64:65, 0:1024]]
    onesl = [wq_s[32:33, 1024:1152], wq_s[64:65, 1024:1152]]

    psA = Et(nc.psum_tensor("psA", [128, 1024], F32))
    psB = Et(nc.psum_tensor("psB", [128, 1024], F32))
    psPV = Et(nc.psum_tensor("psPV", [128, 1024], F32))
    psTf = Et(nc.psum_tensor("psTf", [128, 512], F32))
    psTb = Et(nc.psum_tensor("psTb", [128, 1024], BF16))

    names = [
        "a_id", "a_zm", "a_w", "a_gq", "a_tgq", "a_gqe", "a_x", "a_vs", "a_ss",
        "a_vln", "a_t1", "a_ttp", "a_se", "a_tk", "a_sk", "a_tv", "a_svv", "a_z",
        "c_x", "c_ep", "c_t1", "c_vln", "c_vs", "c_ss", "c_se1", "c_tq", "c_sq",
        "c_ts", "c_vmax", "c_sexp", "c_tpt", "c_spt", "c_tpv", "c_vpv",
        "c_st", "c_t3", "c_se3", "c_tg", "c_vg",
        "d_t", "d_s", "d_v", "d_g", "d_dma",
        "e_w1", "e_w2", "e_b1", "e_b2", "e_gth", "e_tei", "e_sei", "e_thq",
        "e_sh", "e_th", "e_teo", "e_veo", "e_sc",
        "f_in", "f_vs", "f_ss", "f_v", "f_out", "hv", "hs",
    ]
    S = {n: Sem(ctx, nc, n) for n in names}

    xbuf = sb_f32[:, 0:1024]
    O_sb = sb_f32[:, 1024:2048]
    xb_b = gt[:, 0:1024]
    xba = [xbuf, xb_b]
    cdmp = gt[:, 1024:2048]

    def lnfmA(r):
        return sbig[:, 0:4096].rearrange("p (a b) -> p a b", b=512)

    P_sl = [sbig[:, 0:1024], sbig[:, 1024:2048]]
    Pt_sl = [sbig[:, 2048:3072], sbig[:, 3072:4096]]
    lnfm_flat = sbig[:, 4096:5120]
    lnfmC = lnfm_flat.rearrange("p (a b) -> p a b", b=128)
    Q_sl = sbig[:, 5120:6144]
    ln3_flat = sbig[:, 6144:7168]
    ln3fm = ln3_flat.rearrange("p (a b) -> p a b", b=128)

    stg_x2 = [stg[:, 0:1024], stg[:, 1024:2048]]
    stg_l3 = stg[:, 2048:3072]

    T_em = em[0:16, 0:4096]
    A_em = aemv[:, :]
    A_r = gt[:, 0:512]
    sel_g = gt[:, 512:1024]
    slots_g = gt[:, 1024:1536]
    msf_g = gt[:, 1536:2048]
    zer_g = gt[:, 2048:2560]
    dmp_g = gt[:, 2560:3072]
    tmA = gt[:, 3072:3584]
    tmB = gt[:, 3584:4096]
    dstAf = gt[:, 2560:2816]
    dstBf = gt[:, 2816:3072]
    idx_em = gt[0:16, 2048:2304]
    gat_em = gt[0:16, 2304:2560]
    fm128 = gt[:, 3072:3200]
    fd128 = gt[:, 3200:3328]
    onesf = gt[0:1, 3328:3456]
    ind_et = sb_f32[0:16, 1024:1152]

    def vgap(v):
        dmy = v.tensor_copy(hz[0:1, 0:1], hz[0:1, 1:2])
        n = S["hv"].inc(dmy, 1)
        v.wait_ge(S["hv"].h, n)

    def sgap(s):
        dmy = s.copy(hz[0:1, 2:3], hz[0:1, 3:4])
        n = S["hs"].inc(dmy, 1)
        s.wait_ge(S["hs"].h, n)

    def ln_stats_v(v, src, dump, vs_sem, ss_sem, ss_val):
        v.reduce_sum(mst[:, 0:1], src, AX.X)
        v.scalar_tensor_tensor(dump, src, 1.0, src, ALU.mult, ALU.mult,
                               accum_out=mst[:, 2:3])
        vgap(v)
        v.tensor_scalar(mst[:, 1:2], mst[:, 0:1], 1.0 / D, None, ALU.mult)
        v.tensor_scalar(mst[:, 3:4], mst[:, 2:3], 1.0 / D, None, ALU.mult)
        vgap(v)
        v.tensor_tensor(mst[:, 11:12], mst[:, 1:2], mst[:, 1:2], ALU.mult)
        vgap(v)
        vr = v.tensor_tensor(mst[:, 3:4], mst[:, 3:4], mst[:, 11:12],
                             ALU.subtract)
        vs_sem.inc(vr, 1)
        v.wait_ge(ss_sem.h, ss_val)
        v.reciprocal(mst[:, 5:6], mst[:, 4:5])
        vgap(v)

    bnsc = iof[:, 16:28]

    def bn_var(v, src):
        v.bn_stats(bnsc[:, 0:6], src[:, 0:512])
        v.bn_stats(bnsc[:, 6:12], src[:, 512:1024])
        vgap(v)
        return v.bn_aggr(mst[:, 1:3], bnsc[:, 0:12])

    with nc.Block() as blk:

        @blk.gpsimd
        def _(g):
            g.iota(iof[:, :], [[1, 128]], channel_multiplier=0,
                   allow_small_or_imprecise_dtypes=True)
            it = g.iota(iop[:, :], [[0, 1]], channel_multiplier=1,
                        allow_small_or_imprecise_dtypes=True)
            S["a_id"].inc(it, 1)
            S["a_w"].inc(g.dma_start(
                wk3[:, :, :], wk_h[:, :].rearrange("(a p) e -> p a e", p=128)), 16)
            S["a_w"].inc(g.dma_start(
                wv3[:, :, :], wv_h[:, :].rearrange("(a p) e -> p a e", p=128)), 16)
            S["a_w"].inc(g.dma_start(
                wq3[:, :, :], wq_h[:, :].rearrange("(a p) e -> p a e", p=128)), 16)
            S["a_gq"].inc(g.dma_start(gq_s, gq_h[:, :]), 16)
            g.wait_ge(S["a_zm"].h, 1)
            for i in range(NT):
                S["a_z"].inc(g.dma_start(
                    acc_h[i * 128:(i + 1) * 128, :], stg[:, 0:1024]), 16)

        @blk.sync
        def _(sy):
            S["a_gq"].inc(sy.dma_start(lt_s[:, :], lt_h[:, :]), 16)
            for i in range(NT):
                if i >= 1:
                    sy.wait_ge(S["a_vln"].h, i)
                S["a_x"].inc(sy.dma_start(
                    xbuf, x_h[i * 128:(i + 1) * 128, :]), 16)

        @blk.vector
        def _(v):
            v.wait_ge(S["a_id"].h, 1)
            v.tensor_scalar(idt_b[:, :], iof[:, 0:128], iop[:, :], None,
                            ALU.is_equal)
            v.tensor_scalar(idt_f[:, :], iof[:, 0:128], iop[:, :], None,
                            ALU.is_equal)
            v.memset(epsb[:, :], EPS)
            zm = v.memset(stg[:, 0:1024], 0.0)
            S["a_zm"].inc(zm, 1)
            for i in range(NT):
                xb = xbuf
                v.wait_ge(S["a_x"].h, 16 * (i + 1))
                ln_stats_v(v, xb, O_sb, S["a_vs"], S["a_ss"], i + 1)
                if i >= 1:
                    v.wait_ge(S["a_t1"].h, i)
                ap = v.tensor_scalar(ln1t[:, :], xb, mst[:, 1:2], mst[:, 5:6],
                                     ALU.subtract, ALU.mult)
                S["a_vln"].inc(ap, 1)

        @blk.scalar
        def _(s):
            s.wait_ge(S["a_tgq"].h, 1)
            ge = s.copy(gqT[:, :], psTb[:, 0:128])
            S["a_gqe"].inc(ge, 1)
            for i in range(NT):
                s.wait_ge(S["a_vs"].h, i + 1)
                st = s.activation(mst[:, 4:5], mst[:, 3:4], AF.Sqrt, bias=epsb[:, :])
                S["a_ss"].inc(st, 1)
                s.wait_ge(S["a_ttp"].h, i + 1)
                r, tt = i // 4, i % 4
                dst = lnfmA(r)
                s.copy(dst[:, 0:4, tt * 128:(tt + 1) * 128],
                       psTb[:, 0:512].rearrange("p (a b) -> p a b", b=128))
                se = s.copy(dst[:, 4:8, tt * 128:(tt + 1) * 128],
                            psTb[:, 512:1024].rearrange("p (a b) -> p a b", b=128))
                S["a_se"].inc(se, 1)
                if tt == 3:
                    for ec in range(8):
                        k = r * 8 + ec
                        s.wait_ge(S["a_tk"].h, k + 1)
                        ke = s.copy(a3[:, ec, r * 512:(r + 1) * 512],
                                    psA[:, (ec % 2) * 512:(ec % 2 + 1) * 512])
                        S["a_sk"].inc(ke, 1)
                    for t2 in range(4):
                        gg = r * 4 + t2
                        s.wait_ge(S["a_tv"].h, gg + 1)
                        ve = s.copy(b3[:, gg, :], psPV[:, :])
                        S["a_svv"].inc(ve, 1)

        @blk.tensor
        def _(t):
            t.wait_ge(S["a_id"].h, 1)
            t.wait_ge(S["a_gq"].h, 32)
            for dc in range(ND):
                tp = t.transpose(psTb[:, dc * 16:(dc + 1) * 16],
                                 gq_s[0:16, dc * 128:(dc + 1) * 128],
                                 idt_b[0:16, 0:16])
            S["a_tgq"].inc(tp, 1)
            t.wait_ge(S["a_w"].h, 32)
            for i in range(NT):
                r, tt = i // 4, i % 4
                t.wait_ge(S["a_vln"].h, i + 1)
                if i == 0:
                    t.wait_ge(S["a_gqe"].h, 1)
                else:
                    t.wait_ge(S["a_se"].h, i)
                for dc in range(ND):
                    tp = t.transpose(psTb[:, dc * 128:(dc + 1) * 128],
                                     ln1t[:, dc * 128:(dc + 1) * 128],
                                     idt_b[:, :])
                S["a_ttp"].inc(tp, 1)
                S["a_t1"].inc(t.nop(), 1)
                if tt == 3:
                    lf = lnfmA(r)
                    t.wait_ge(S["a_se"].h, 4 * (r + 1))
                    for ec in range(8):
                        k = r * 8 + ec
                        if k >= 2:
                            t.wait_ge(S["a_sk"].h, k - 1)
                        dsts = psA[:, (ec % 2) * 512:(ec % 2 + 1) * 512]
                        for dc in range(ND):
                            mm = t.matmul(dsts,
                                          wk3[:, dc, ec * 128:(ec + 1) * 128],
                                          lf[:, dc, :], start=(dc == 0),
                                          stop=(dc == 7))
                        S["a_tk"].inc(mm, 1)
                    for t2 in range(4):
                        gg = r * 4 + t2
                        if gg >= 1:
                            t.wait_ge(S["a_svv"].h, gg)
                        for dc in range(ND):
                            t.matmul(psPV[:, 0:512],
                                     lf[:, dc, t2 * 128:(t2 + 1) * 128],
                                     wv3[:, dc, 0:512],
                                     start=(dc == 0), stop=(dc == 7))
                            mm = t.matmul(psPV[:, 512:1024],
                                          lf[:, dc, t2 * 128:(t2 + 1) * 128],
                                          wv3[:, dc, 512:1024],
                                          start=(dc == 0), stop=(dc == 7))
                        S["a_tv"].inc(mm, 1)

    if upto < 1:
        if dbg:
            wdbg2 = nc.dram_tensor("wdbg2", [128, ND * D], BF16,
                                   kind="ExternalOutput")
            ldbg = nc.dram_tensor("ldbg", [128, D], BF16, kind="ExternalOutput")
            mdbg = nc.dram_tensor("mdbg", [128, 16], F32, kind="ExternalOutput")
            with nc.Block() as blk:
                @blk.sync
                def _(sy):
                    sy.wait_ge(S["a_vln"].h, NT)
                    sy.dma_start(wdbg2[:, :], wk_s[:, :]).then_inc(S["d_dma"].h, 16)
                    sy.dma_start(ldbg[:, :], ln1t).then_inc(S["d_dma"].h, 16)
                    sy.dma_start(mdbg[:, :], mst[:, 0:16]).then_inc(S["d_dma"].h, 16)
                    sy.wait_ge(S["d_dma"].h, S["d_dma"].v + 48)
            S["d_dma"].v += 48
        ctx.close()
        nc.compile()
        return nc
    with nc.Block() as blk:

        @blk.sync
        def _(sy):
            if dbg:
                sy.wait_ge(S["a_sk"].h, 64)
                sy.wait_ge(S["a_svv"].h, 32)
                sy.dma_start(kdbg_h[:, :], ar_a[:, :]).then_inc(S["d_dma"].h, 16)
                sy.dma_start(vdbg_h[:, :], ar_b[:, :]).then_inc(S["d_dma"].h, 16)
                sy.wait_ge(S["d_dma"].h, 32)
                S["d_dma"].v += 32
            sy.wait_ge(S["a_vln"].h, 32)
            S["c_x"].inc(sy.dma_start(xba[0], x_h[0:128, :]), 16)
            sy.wait_ge(S["a_tk"].h, 64)
            S["c_x"].inc(sy.dma_start(xba[1], x_h[128:256, :]), 16)
            for i in range(NT):
                sy.wait_ge(S["c_ep"].h, i + 1)
                if i + 2 < NT:
                    S["c_x"].inc(sy.dma_start(
                        xba[i % 2], x_h[(i + 2) * 128:(i + 3) * 128, :]), 16)
                S["c_st"].inc(sy.dma_start(
                    x2_h[i * 128:(i + 1) * 128, :], stg_x2[i % 2]), 16)
                S["c_st"].inc(sy.dma_start(
                    ln3_h[i * 128:(i + 1) * 128, :], stg_l3), 16)

        @blk.vector
        def _(v):
            v.wait_ge(S["c_x"].h, 16)
            S["c_vs"].inc(bn_var(v, xbuf), 1)
            v.wait_ge(S["c_ss"].h, 1)
            v.reciprocal(mst[:, 5:6], mst[:, 4:5])
            vgap(v)
            v.wait_ge(S["a_t1"].h, NT)
            ap = v.tensor_scalar(ln1t[:, :], xbuf, mst[:, 1:2], mst[:, 5:6],
                                 ALU.subtract, ALU.mult)
            S["c_vln"].inc(ap, 1)
            for i in range(NT):
                xb = xba[i % 2]
                for c in range(NCH):
                    ps = psA if c % 2 == 0 else psB
                    v.wait_ge(S["c_ts"].h, 4 * i + c + 1)
                    v.reduce_max(mst[:, 11:12], ps[:, 0:1024], AX.X)
                    vgap(v)
                    if c == 0:
                        v.tensor_copy(mst[:, 6:7], mst[:, 11:12])
                        vgap(v)
                        bx = v.tensor_scalar(mst[:, 10:11], mst[:, 6:7],
                                             -SCALE, None, ALU.mult)
                    else:
                        v.tensor_tensor(mst[:, 8:9], mst[:, 11:12],
                                        mst[:, 6:7], ALU.max)
                        vgap(v)
                        v.tensor_tensor(mst[:, 9:10], mst[:, 6:7], mst[:, 8:9],
                                        ALU.subtract)
                        v.tensor_scalar(mst[:, 10:11], mst[:, 8:9], -SCALE,
                                        None, ALU.mult)
                        v.tensor_copy(mst[:, 6:7], mst[:, 8:9])
                        vgap(v)
                        bx = v.tensor_scalar(mst[:, 9:10], mst[:, 9:10],
                                             SCALE, None, ALU.mult)
                    S["c_vmax"].inc(bx, 1)
                    if c == 3 and i + 1 < NT:
                        v.wait_ge(S["c_x"].h, 16 * (i + 2))
                        S["c_vs"].inc(bn_var(v, xba[(i + 1) % 2]), 1)
                    v.wait_ge(S["c_sexp"].h, 4 * i + c + 1)
                    if c == 0:
                        v.tensor_copy(mst[:, 7:8], mst[:, 13:14])
                    else:
                        v.scalar_tensor_tensor(mst[:, 7:8], mst[:, 7:8],
                                               mst[:, 9:10], mst[:, 13:14],
                                               ALU.mult, ALU.add)
                    if c == 3 and i + 1 < NT:
                        v.wait_ge(S["c_ss"].h, 3 * i + 2)
                        v.reciprocal(mst[:, 5:6], mst[:, 4:5])
                        vgap(v)
                        v.wait_ge(S["c_t1"].h, i + 1)
                        ap = v.tensor_scalar(ln1t[:, :], xba[(i + 1) % 2],
                                             mst[:, 1:2], mst[:, 5:6],
                                             ALU.subtract, ALU.mult)
                        S["c_vln"].inc(ap, 1)
                    v.wait_ge(S["c_tpv"].h, 4 * i + c + 1)
                    if c == 0:
                        pv = v.tensor_copy(O_sb[:, 0:1024], psPV[:, 0:1024])
                    else:
                        pv = v.scalar_tensor_tensor(
                            O_sb[:, 0:1024], O_sb[:, 0:1024], mst[:, 9:10],
                            psPV[:, 0:1024], ALU.mult, ALU.add)
                    S["c_vpv"].inc(pv, 1)
                p2 = 3 * i + 3 if i + 1 < NT else 3 * i + 2
                v.reciprocal(mst[:, 15:16], mst[:, 7:8])
                vgap(v)
                v.tensor_tensor(mst[:, 22:23], mst[:, 15:16], mst[:, 15:16],
                                ALU.mult)
                bn_var(v, O_sb)
                vgap(v)
                vn = v.tensor_tensor(mst[:, 2:3], mst[:, 2:3], mst[:, 22:23],
                                     ALU.mult)
                S["c_vs"].inc(vn, 1)
                v.wait_ge(S["c_ss"].h, p2)
                v.reciprocal(mst[:, 5:6], mst[:, 4:5])
                vgap(v)
                v.tensor_scalar(O_sb, O_sb, mst[:, 1:2], mst[:, 5:6],
                                ALU.subtract, ALU.mult)
                vgap(v)
                v.scalar_tensor_tensor(xb, O_sb, mst[:, 15:16], xb,
                                       ALU.mult, ALU.add)
                vgap(v)
                S["c_vs"].inc(bn_var(v, xb), 1)
                v.wait_ge(S["c_ss"].h, p2 + 1)
                v.reciprocal(mst[:, 5:6], mst[:, 4:5])
                vgap(v)
                if i < 2:
                    v.wait_ge(S["a_z"].h, 16 * NT)
                if i >= 1:
                    v.wait_ge(S["c_st"].h, 32 * i)
                    v.wait_ge(S["c_t3"].h, i)
                v.tensor_copy(stg_x2[i % 2], xb)
                ep = v.tensor_scalar(stg_l3, xb, mst[:, 1:2], mst[:, 5:6],
                                     ALU.subtract, ALU.mult)
                S["c_ep"].inc(ep, 1)

        @blk.scalar
        def _(s):

            def sqrt_s(val):
                s.wait_ge(S["c_vs"].h, val)
                r = s.activation(mst[:, 4:5], mst[:, 2:3], AF.Sqrt,
                                 bias=epsb[:, :])
                S["c_ss"].inc(r, 1)

            for i in range(NT):
                if i == 0:
                    sqrt_s(1)
                    s.preload_act_table(AF.Exp)
                s.wait_ge(S["c_t1"].h, i + 1)
                le = s.copy(lnfm_flat, psTb[:, 0:1024])
                S["c_se1"].inc(le, 1)
                for ec in range(8):
                    s.wait_ge(S["c_tq"].h, 8 * i + ec + 1)
                    qe = s.copy(Q_sl[:, ec * 128:(ec + 1) * 128],
                                psPV[:, (ec % 2) * 512:(ec % 2) * 512 + 128])
                    S["c_sq"].inc(qe, 1)
                if i >= 1:
                    sqrt_s(3 * i)
                    sqrt_s(3 * i + 1)
                    s.preload_act_table(AF.Exp)
                for c in range(NCH):
                    ps = psA if c % 2 == 0 else psB
                    s.wait_ge(S["c_vmax"].h, 4 * i + c + 1)
                    if c > 0:
                        s.activation(mst[:, 9:10], mst[:, 9:10], AF.Exp)
                    ex = s.activation(P_sl[c % 2][:, 0:1024], ps[:, 0:1024],
                                      AF.Exp, bias=mst[:, 10:11], scale=SCALE,
                                      accum_out=mst[:, 13:14])
                    S["c_sexp"].inc(ex, 1)
                    s.wait_ge(S["c_tpt"].h, 4 * i + c + 1)
                    pe = s.copy(Pt_sl[c % 2], psTb[:, 0:1024])
                    S["c_spt"].inc(pe, 1)
                s.preload_act_table(AF.Sqrt)
                if i + 1 < NT:
                    sqrt_s(3 * i + 2)
                if i >= 1:
                    s.wait_ge(S["c_t3"].h, i)
                    l3 = s.copy(ln3_flat, psTb[:, 0:1024])
                    S["c_se3"].inc(l3, 1)
                if i >= 2:
                    s.wait_ge(S["c_tg"].h, i - 1)
                    gv = s.copy(g_em[:, (i - 2) * 128:(i - 1) * 128],
                                psTf[0:16, 0:128])
                    S["c_vg"].inc(gv, 1)
            sqrt_s(3 * NT - 1)
            sqrt_s(3 * NT)
            s.wait_ge(S["c_t3"].h, NT)
            l3 = s.copy(ln3_flat, psTb[:, 0:1024])
            S["c_se3"].inc(l3, 1)
            s.wait_ge(S["c_tg"].h, NT - 1)
            gv = s.copy(g_em[:, (NT - 2) * 128:(NT - 1) * 128],
                        psTf[0:16, 0:128])
            S["c_vg"].inc(gv, 1)
            s.wait_ge(S["c_tg"].h, NT)
            gv = s.copy(g_em[:, (NT - 1) * 128:NT * 128], psTf[0:16, 0:128])
            S["c_vg"].inc(gv, 1)

        @blk.tensor
        def _(t):
            t.wait_ge(S["a_sk"].h, 64)
            t.wait_ge(S["a_svv"].h, 32)
            t.wait_ge(S["a_w"].h, 48)

            def g_mm(j):
                t.wait_ge(S["c_se3"].h, j + 1)
                t.wait_ge(S["a_gqe"].h, 1)
                if j >= 1:
                    t.wait_ge(S["c_vg"].h, j)
                for dc in range(ND):
                    gm = t.matmul(psTf[0:16, 0:128],
                                  gqT[:, dc * 16:(dc + 1) * 16],
                                  ln3fm[:, dc, :], start=(dc == 0),
                                  stop=(dc == 7))
                S["c_tg"].inc(gm, 1)

            for i in range(NT):
                t.wait_ge(S["c_vln"].h, i + 1)
                if i == 0:
                    t.wait_ge(S["a_se"].h, 32)
                elif i == 1:
                    t.wait_ge(S["c_spt"].h, 4)
                else:
                    t.wait_ge(S["c_se3"].h, i - 1)
                for dc in range(ND):
                    tp = t.transpose(psTb[:, dc * 128:(dc + 1) * 128],
                                     ln1t[:, dc * 128:(dc + 1) * 128],
                                     idt_b[:, :])
                S["c_t1"].inc(tp, 1)
                t.wait_ge(S["c_se1"].h, i + 1)
                if i >= 1:
                    t.wait_ge(S["c_vpv"].h, 4 * i)
                for ec in range(8):
                    if ec >= 2:
                        t.wait_ge(S["c_sq"].h, 8 * i + ec - 1)
                    dq = psPV[:, (ec % 2) * 512:(ec % 2) * 512 + 128]
                    for dc in range(ND):
                        mm = t.matmul(dq, wq3[:, dc, ec * 128:(ec + 1) * 128],
                                      lnfmC[:, dc, :], start=(dc == 0),
                                      stop=(dc == 7))
                    S["c_tq"].inc(mm, 1)

                def s_mm(c):
                    ps = psA if c % 2 == 0 else psB
                    if 4 * i + c >= 2:
                        t.wait_ge(S["c_sexp"].h, 4 * i + c - 1)
                    for ec in range(8):
                        lt_ = Q_sl[:, ec * 128:(ec + 1) * 128]
                        t.matmul(ps[:, 0:512], lt_,
                                 a3[:, ec, c * 1024:c * 1024 + 512],
                                 start=(ec == 0), stop=(ec == 7))
                        mm = t.matmul(ps[:, 512:1024], lt_,
                                      a3[:, ec, c * 1024 + 512:(c + 1) * 1024],
                                      start=(ec == 0), stop=(ec == 7))
                    S["c_ts"].inc(mm, 1)

                s_mm(0)
                s_mm(1)
                if i >= 2:
                    g_mm(i - 2)
                for c in range(NCH):
                    t.wait_ge(S["c_sexp"].h, 4 * i + c + 1)
                    if c > 0:
                        t.wait_ge(S["c_spt"].h, 4 * i + c)
                    for sub in range(8):
                        tp = t.transpose(psTb[:, sub * 128:(sub + 1) * 128],
                                         P_sl[c % 2][:, sub * 128:(sub + 1) * 128],
                                         idt_b[:, :])
                    S["c_tpt"].inc(tp, 1)
                    t.wait_ge(S["c_spt"].h, 4 * i + c + 1)
                    if c == 0:
                        t.wait_ge(S["c_sq"].h, 8 * (i + 1))
                    else:
                        t.wait_ge(S["c_vpv"].h, 4 * i + c)
                    for sub in range(8):
                        lt_ = Pt_sl[c % 2][:, sub * 128:(sub + 1) * 128]
                        t.matmul(psPV[:, 0:512], lt_, b3[:, c * 8 + sub, 0:512],
                                 start=(sub == 0), stop=(sub == 7))
                        mm = t.matmul(psPV[:, 512:1024], lt_,
                                      b3[:, c * 8 + sub, 512:1024],
                                      start=(sub == 0), stop=(sub == 7))
                    S["c_tpv"].inc(mm, 1)
                    if c + 2 < NCH:
                        s_mm(c + 2)
                if i >= 1:
                    t.wait_ge(S["c_ep"].h, i)
                    t.wait_ge(S["c_spt"].h, 4 * (i + 1))
                    for dc in range(ND):
                        tp = t.transpose(psTb[:, dc * 128:(dc + 1) * 128],
                                         stg_l3[:, dc * 128:(dc + 1) * 128],
                                         idt_b[:, :])
                    S["c_t3"].inc(tp, 1)
            t.wait_ge(S["c_ep"].h, NT)
            t.wait_ge(S["c_spt"].h, 4 * NT)
            for dc in range(ND):
                tp = t.transpose(psTb[:, dc * 128:(dc + 1) * 128],
                                 stg_l3[:, dc * 128:(dc + 1) * 128],
                                 idt_b[:, :])
            S["c_t3"].inc(tp, 1)
            g_mm(NT - 2)
            g_mm(NT - 1)

    if upto < 2:
        ctx.close()
        nc.compile()
        return nc
    with nc.Block() as blk:
        d0s = S["d_s"].v
        d0t = S["d_t"].v
        d0v = S["d_v"].v
        d0g = S["d_g"].v

        @blk.scalar
        def _(s):
            s.wait_ge(S["a_gq"].h, 16)
            rv = s.activation(rt1[:, :], lt_s[:, :], AF.Exp, scale=-1.0)
            S["d_s"].inc(rv, 1)

        @blk.tensor
        def _(t):
            t.wait_ge(S["d_s"].h, d0s + 1)
            t.wait_ge(S["d_v"].h, d0v + 1)
            t.wait_ge(S["c_sexp"].h, 4 * NT)
            t.wait_ge(S["c_se3"].h, NT)
            bm = t.matmul(psTf[0:16, 0:1], onesf[0:1, 0:16], rt1[:, :],
                          start=True, stop=True)
            S["d_t"].inc(bm, 1)
            t.wait_ge(S["d_g"].h, d0g + 1)
            t.transpose(psTf[:, 64:65], tmB[0:1, 0:128], idt_f[0:1, 0:1])
            tr = t.transpose(psTf[:, 65:66], tmB[0:1, 128:256], idt_f[0:1, 0:1])
            S["d_t"].inc(tr, 1)

        @blk.vector
        def _(v):
            v.wait_ge(S["c_vg"].h, NT)
            v.memset(onesf[:, :], 1.0)
            v.memset(onesl[0], 1.0)
            om = v.memset(onesl[1], 1.0)
            S["d_v"].inc(om, 1)
            v.wait_ge(S["d_t"].h, d0t + 2)
            v.wait_ge(S["d_g"].h, d0g + 1)
            v.tensor_copy(rt16[:, :], psTf[0:16, 0:1])
            v.tensor_copy(pmod[:, :], psTf[:, 64:65])
            v.tensor_copy(pdiv[:, :], psTf[:, 65:66])
            vgap(v)
            v.tensor_scalar(ind_e[:, :], iof[:, 0:16], pmod[:, :], None,
                            ALU.is_equal)
            v.tensor_scalar(carry_m[:, :], fm128[:, :], pmod[:, :], None,
                            ALU.is_equal)
            v.tensor_scalar(dmp_g[:, 0:128], fd128[:, :], pdiv[:, :], None,
                            ALU.is_gt)
            vgap(v)
            v.tensor_tensor(carry_m[:, :], carry_m[:, :], dmp_g[:, 0:128],
                            ALU.mult)
            v.memset(zer_g, 0.0)
            v.memset(lo_s[:, :], 0.0)
            v.memset(hi_s[:, :], 1.0)
            v.tensor_scalar(T_em, g_em[:, :], rt16[:, :], None, ALU.mult)
            te = v.tensor_scalar(ind_et[:, :], fm128[0:16, :], iop[0:16, :],
                                 None, ALU.is_equal)
            S["d_v"].inc(te, 1)

        @blk.gpsimd
        def _(g):
            g.wait_ge(S["a_tk"].h, 64)
            g.wait_ge(S["c_tg"].h, NT)
            g.iota(fm128[:, :].rearrange("p (a b) -> p a b", b=16),
                   [[0, 8], [1, 16]], channel_multiplier=0,
                   allow_small_or_imprecise_dtypes=True)
            g.iota(fd128[:, :].rearrange("p (a b) -> p a b", b=16),
                   [[1, 8], [0, 16]], channel_multiplier=0,
                   allow_small_or_imprecise_dtypes=True)
            g.iota(tmB[0:1, 0:128].rearrange("p (a b) -> p a b", b=16),
                   [[0, 8], [1, 16]], channel_multiplier=0,
                   allow_small_or_imprecise_dtypes=True)
            g.iota(tmB[0:1, 128:256].rearrange("p (a b) -> p a b", b=16),
                   [[1, 8], [0, 16]], channel_multiplier=0,
                   allow_small_or_imprecise_dtypes=True)
            it = g.iota(tokid[:, :], [[1, 512]], channel_multiplier=0,
                        allow_small_or_imprecise_dtypes=True)
            S["d_g"].inc(it, 1)

    for it_ in range(SINK):
        with nc.Block() as blk:
            t0 = S["d_t"].v
            v0 = S["d_v"].v
            s0 = S["d_s"].v

            @blk.tensor
            def _(t):
                t.wait_ge(S["d_v"].h, v0)
                for j in range(32):
                    tp = t.transpose(psTf[:, j * 16:(j + 1) * 16],
                                     T_em[0:16, j * 128:(j + 1) * 128],
                                     idt_f[0:16, 0:16])
                S["d_t"].inc(tp, 1)
                t.wait_ge(S["d_v"].h, v0 + 3)
                for rr in range(4):
                    ps = psA if rr % 2 == 0 else psB
                    if rr >= 2:
                        t.wait_ge(S["d_s"].h, s0 + rr + 1)
                    for jj in range(8):
                        j = rr * 8 + jj
                        tp = t.transpose(ps[0:16, jj * 128:(jj + 1) * 128],
                                         tmA[:, j * 16:(j + 1) * 16],
                                         idt_f[:, :])
                    S["d_t"].inc(tp, 1)

            @blk.vector
            def _(v):
                v.wait_ge(S["d_t"].h, t0 + 1)
                v.tensor_copy(tmB, psTf[:, 0:512])
                vgap(v)
                v3 = tmB.rearrange("p (a b) -> p a b", b=16)
                v.tensor_reduce(seg32[:, :], v3, AX.X, ALU.max)
                vgap(v)
                sb = v.tensor_tensor(
                    tmA.rearrange("p (a b) -> p a b", b=16), v3,
                    seg32[:, :, None].to_broadcast((128, 32, 16)), ALU.subtract)
                S["d_v"].inc(sb, 1)
                v.wait_ge(S["d_s"].h, s0 + 1)
                e3 = dmp_g.rearrange("p (a b) -> p a b", b=16)
                sm = v.tensor_reduce(seg32[:, :], e3, AX.X, ALU.add)
                S["d_v"].inc(sm, 1)
                v.wait_ge(S["d_s"].h, s0 + 2)
                sb2 = v.tensor_tensor(
                    tmA.rearrange("p (a b) -> p a b", b=16),
                    tmA.rearrange("p (a b) -> p a b", b=16),
                    seg32[:, :, None].to_broadcast((128, 32, 16)), ALU.subtract)
                S["d_v"].inc(sb2, 1)
                v.wait_ge(S["d_s"].h, s0 + 6)
                v.reduce_max(m16[:, :], T_em, AX.X)
                vgap(v)
                nm = v.tensor_scalar(nm16[:, :], m16[:, :], -1.0, None, ALU.mult)
                S["d_v"].inc(nm, 1)
                v.wait_ge(S["d_s"].h, s0 + 8)
                v.tensor_tensor(m16[:, :], m16[:, :], rs16[:, :], ALU.add)
                vgap(v)
                ts_ = v.tensor_scalar(T_em, T_em, m16[:, :], None, ALU.subtract)
                S["d_v"].inc(ts_, 1)

            @blk.scalar
            def _(s):
                s.wait_ge(S["d_v"].h, v0 + 1)
                ex = s.activation(dmp_g, tmA, AF.Exp)
                S["d_s"].inc(ex, 1)
                s.wait_ge(S["d_v"].h, v0 + 2)
                ln = s.activation(seg32[:, :], seg32[:, :], AF.Ln)
                S["d_s"].inc(ln, 1)
                for rr in range(4):
                    ps = psA if rr % 2 == 0 else psB
                    s.wait_ge(S["d_t"].h, t0 + 2 + rr)
                    ev = s.copy(T_em[0:16, rr * 1024:(rr + 1) * 1024],
                                ps[0:16, 0:1024])
                    S["d_s"].inc(ev, 1)
                s.wait_ge(S["d_v"].h, v0 + 4)
                ex2 = s.activation(A_em, T_em, AF.Exp, bias=nm16[:, :],
                                   accum_out=rs16[:, :])
                S["d_s"].inc(ex2, 1)
                sgap(s)
                ln2 = s.activation(rs16[:, :], rs16[:, :], AF.Ln)
                S["d_s"].inc(ln2, 1)

    with nc.Block() as blk:
        d7s = S["d_s"].v
        d7v = S["d_v"].v
        d7d = S["d_dma"].v

        @blk.scalar
        def _(s):
            s.wait_ge(S["d_v"].h, d7v)
            fa = s.activation(A_em, T_em, AF.Exp)
            S["d_s"].inc(fa, 1)

        @blk.sync
        def _(sy):
            sy.wait_ge(S["d_s"].h, d7s + 1)
            dm = sy.dma_start(adram[:, :], A_em[:, :])
            S["d_dma"].inc(dm, 16)
            sy.wait_ge(S["d_dma"].h, d7d + 16)
            dm = sy.dma_start(
                A_r, bass.AP(adram, 0, [[512, 8], [4096, 16], [1, 512]]))
            S["d_dma"].inc(dm, 16)

        @blk.vector
        def _(v):
            v.wait_ge(S["d_dma"].h, d7d + 32)
            v.tensor_copy(a_bf[:, :], A_r)
            tf = v.tensor_scalar(toff[:, :], pdiv[:, :], 512.0, None, ALU.mult)
            S["d_v"].inc(tf, 1)

        @blk.gpsimd
        def _(g):
            g.wait_ge(S["d_v"].h, d7v + 1)
            tk = g.tensor_scalar(tokid[:, :], tokid[:, :], toff[:, :], None,
                                 ALU.add)
            S["d_g"].inc(tk, 1)

    with nc.Block() as blk:
        b_t0 = S["d_t"].v
        b_v0 = S["d_v"].v
        b_g0 = S["d_g"].v

        @blk.vector
        def _(v):
            for k in range(BIS):
                if k > 0:
                    v.wait_ge(S["d_t"].h, b_t0 + 2 * k)
                    v.tensor_copy(cb_s[:, :], psTf[:, 0:1])
                    vgap(v)
                    v.tensor_scalar(msk_s[:, :], cb_s[:, :], float(CAP), None,
                                    ALU.is_ge)
                    v.tensor_scalar(nmsk_s[:, :], cb_s[:, :], float(CAP), None,
                                    ALU.is_lt)
                    vgap(v)
                    v.copy_predicated(lo_s[:, :], msk_s[:, :], mid_s[:, :])
                    v.copy_predicated(hi_s[:, :], nmsk_s[:, :], mid_s[:, :])
                    vgap(v)
                v.tensor_tensor(mid_s[:, :], lo_s[:, :], hi_s[:, :], ALU.add)
                vgap(v)
                v.tensor_scalar(mid_s[:, :], mid_s[:, :], 0.5, None, ALU.mult)
                vgap(v)
                ct = v.tensor_scalar(dmp_g, A_r, mid_s[:, :], 0.0, ALU.is_ge,
                                     ALU.add, accum_out=cnt_p[:, :])
                S["d_v"].inc(ct, 1)
                v.wait_ge(S["d_t"].h, b_t0 + 2 * k + 1)
                c1 = v.tensor_copy(c16[:, :], psTf[0:16, 4:5])
                S["d_v"].inc(c1, 1)
            v.wait_ge(S["d_t"].h, b_t0 + 2 * BIS)
            v.tensor_copy(cb_s[:, :], psTf[:, 0:1])
            vgap(v)
            v.tensor_scalar(msk_s[:, :], cb_s[:, :], float(CAP), None, ALU.is_ge)
            vgap(v)
            fu = v.copy_predicated(lo_s[:, :], msk_s[:, :], mid_s[:, :])
            S["d_v"].inc(fu, 1)

        @blk.tensor
        def _(t):
            for k in range(BIS):
                t.wait_ge(S["d_v"].h, b_v0 + 2 * k + 1)
                m1 = t.matmul(psTf[0:16, 4:5], ind_e[:, :], cnt_p[:, :],
                              start=True, stop=True)
                S["d_t"].inc(m1, 1)
                t.wait_ge(S["d_v"].h, b_v0 + 2 * k + 2)
                m2 = t.matmul(psTf[:, 0:1], ind_et[:, :], c16[:, :],
                              start=True, stop=True)
                S["d_t"].inc(m2, 1)

    with nc.Block() as blk:
        d9t = S["d_t"].v
        d9v = S["d_v"].v

        @blk.vector
        def _(v):
            vgap(v)
            v.tensor_scalar(sel_g, A_r, lo_s[:, :], None, ALU.is_ge)
            vgap(v)
            sc = v.tensor_tensor_scan(slots_g, sel_g, zer_g, 0.0, ALU.add,
                                      ALU.add)
            S["d_v"].inc(sc, 1)
            v.wait_ge(S["d_t"].h, d9t + 1)
            v.tensor_copy(cb_s[:, :], psTf[:, 0:1])
            vgap(v)
            v.tensor_scalar(slots_g, slots_g, cb_s[:, :], None, ALU.add)
            vgap(v)
            v.tensor_tensor(msf_g, slots_g, sel_g, ALU.mult)
            vgap(v)
            v.tensor_scalar(dmp_g, msf_g, float(CAP) + 0.5, None, ALU.is_le)
            vgap(v)
            v.tensor_tensor(msf_g, msf_g, dmp_g, ALU.mult)
            vgap(v)
            mi = v.tensor_scalar(msi[:, :], msf_g, 1.0, None, ALU.subtract)
            S["d_v"].inc(mi, 1)

        @blk.tensor
        def _(t):
            t.wait_ge(S["d_v"].h, d9v + 1)
            cm = t.matmul(psTf[:, 0:1], carry_m[:, :], slots_g[:, 511:512],
                          start=True, stop=True)
            S["d_t"].inc(cm, 1)

    with nc.Block() as blk:
        dAv = S["d_v"].v
        dAg = S["d_g"].v

        @blk.gpsimd
        def _(g):
            g.wait_ge(S["d_v"].h, dAv)
            l1 = g.local_scatter(dstA[:, :], tokid[:, :], msi[:, :], 128, CAP,
                                 512)
            S["d_g"].inc(l1, 1)
            l2 = g.local_scatter(dstB[:, :], a_bf[:, :], msi[:, :], 128, CAP,
                                 512)
            S["d_g"].inc(l2, 1)

    with nc.Block() as blk:
        dBt = S["d_t"].v
        dBv = S["d_v"].v
        dBg = S["d_g"].v

        @blk.vector
        def _(v):
            v.wait_ge(S["d_g"].h, dBg)
            v.tensor_copy(dstAf[:, :], dstA[:, :])
            cpy = v.tensor_copy(dstBf[:, :], dstB[:, :])
            S["d_v"].inc(cpy, 1)
            v.wait_ge(S["d_t"].h, dBt + 1)
            v.tensor_copy(idx_em[:, :], psTf[0:16, 0:256])
            cpy2 = v.tensor_copy(gat_em[:, :], psTf[0:16, 256:512])
            S["d_v"].inc(cpy2, 1)
            v.wait_ge(S["d_t"].h, dBt + 2)
            v.tensor_copy(idx_f[:, :], psA[:, 0:32])
            v.tensor_copy(gat_c[:, :], psA[:, 32:64])
            vgap(v)
            iv = v.tensor_copy(idx_i[:, :], idx_f[:, :])
            S["d_v"].inc(iv, 1)

        @blk.tensor
        def _(t):
            t.wait_ge(S["d_v"].h, dBv + 1)
            t.matmul(psTf[0:16, 0:256], ind_e[:, :], dstAf[:, :], start=True,
                     stop=True)
            mm = t.matmul(psTf[0:16, 256:512], ind_e[:, :], dstBf[:, :],
                          start=True, stop=True)
            S["d_t"].inc(mm, 1)
            t.wait_ge(S["d_v"].h, dBv + 2)
            for hh in range(2):
                t.transpose(psA[:, hh * 16:(hh + 1) * 16],
                            idx_em[0:16, hh * 128:(hh + 1) * 128],
                            idt_f[0:16, 0:16])
                tp = t.transpose(psA[:, 32 + hh * 16:32 + (hh + 1) * 16],
                                 gat_em[0:16, hh * 128:(hh + 1) * 128],
                                 idt_f[0:16, 0:16])
            S["d_t"].inc(tp, 1)

    dv_done = S["d_v"].v

    if dbg:
        with nc.Block() as blk:
            dgv = S["d_v"].v
            dgd = S["d_dma"].v

            @blk.sync
            def _(sy):
                sy.wait_ge(S["d_v"].h, dgv)
                S["d_dma"].inc(sy.dma_start(gdbg_h[:, :], g_em[:, :]), 16)
                S["d_dma"].inc(sy.dma_start(adbg_h[:, :], A_r), 16)
                S["d_dma"].inc(sy.dma_start(idbg_h[:, :], idx_f[:, :]), 16)
                S["d_dma"].inc(sy.dma_start(wdbg_h[:, :], gat_c[:, :]), 16)
                sy.wait_ge(S["d_dma"].h, dgd + 64)

    if upto < 3:
        ctx.close()
        nc.compile()
        return nc
    NEL = ne_lim
    with nc.Block() as blk:

        @blk.sync
        def _(sy):
            for e in range(NEL):
                if e >= 2:
                    sy.wait_ge(S["e_tei"].h, 2 * e - 2)
                bl = sy.dma_start(
                    b1raw[:, (e % 2) * 128:(e % 2 + 1) * 128],
                    b1_h[e:e + 1, :].rearrange("o (a p) -> (o a) p", p=128))
                S["e_b1"].inc(bl, 16)

        @blk.gpsimd
        def _(g):
            g.wait_ge(S["c_st"].h, 16 * 2 * NT)
            g.wait_ge(S["c_t3"].h, NT)
            g.wait_ge(S["a_z"].h, 16 * NT)
            g.wait_ge(S["d_v"].h, dv_done)
            for e in range(NEL):
                if e == 0:
                    g.wait_ge(S["c_ts"].h, 4 * NT)
                else:
                    g.wait_ge(S["e_th"].h, e)
                wl = g.dma_start(
                    a3[:, :, :],
                    w1_h[e, :, :].rearrange("(a p) f -> p a f", p=128))
                S["e_w1"].inc(wl, 16)
                if e >= 2:
                    g.wait_ge(S["e_teo"].h, 2 * (e - 1))
                b2l = g.dma_start(b2sl[e % 2], b2_h[e:e + 1, :])
                S["e_b2"].inc(b2l, 16)
                if e_stage >= 1:
                    if e >= 1:
                        g.wait_ge(S["e_tei"].h, 2 * e)
                    for hh in range(2):
                        ga = g.indirect_dma_start(
                            ei_sm[:, hh, :], None, ln3_h[:, :],
                            IndirectOffsetOnAxis(
                                idx_i[:, hh * 16 + e:hh * 16 + e + 1], 0))
                        S["e_gth"].inc(ga, 16)
                if e >= 1 and e_stage >= 5:
                    ep = e - 1
                    for hh in range(2):
                        g.wait_ge(S["e_veo"].h, 2 * ep + hh + 1)
                        g.wait_ge(S["e_sc"].h, 16 * (2 * ep + hh))
                        sc = g.indirect_dma_start(
                            acc_h[:, :],
                            IndirectOffsetOnAxis(
                                idx_i[:, hh * 16 + ep:hh * 16 + ep + 1], 0),
                            sb_f32[:, hh * 1024:(hh + 1) * 1024], None,
                            compute_op=ALU.add)
                        S["e_sc"].inc(sc, 16)
                if e == 0:
                    g.wait_ge(S["c_tpv"].h, 4 * NT)
                else:
                    g.wait_ge(S["e_teo"].h, 2 * e)
                w2l = g.dma_start(
                    b3[:, :, :],
                    w2_h[e, :, :].rearrange("(a p) d -> p a d", p=128))
                S["e_w2"].inc(w2l, 16)
            e = NEL - 1
            for hh in range(2):
                if e_stage < 5:
                    break
                g.wait_ge(S["e_veo"].h, 2 * e + hh + 1)
                g.wait_ge(S["e_sc"].h, 16 * (2 * e + hh))
                sc = g.indirect_dma_start(
                    acc_h[:, :],
                    IndirectOffsetOnAxis(
                        idx_i[:, hh * 16 + e:hh * 16 + e + 1], 0),
                    sb_f32[:, hh * 1024:(hh + 1) * 1024], None,
                    compute_op=ALU.add)
                S["e_sc"].inc(sc, 16)

        @blk.tensor
        def _(t):
            for e in range(NEL):
                if e_stage < 2:
                    break
                t.wait_ge(S["e_gth"].h, 32 * (e + 1))
                if e == 0:
                    t.wait_ge(S["c_vg"].h, NT)
                    t.wait_ge(S["c_se3"].h, NT)
                    t.wait_ge(S["d_v"].h, dv_done)
                for ch in range(2):
                    if ch == 0:
                        if e >= 1:
                            t.wait_ge(S["e_sei"].h, 2 * e)
                    else:
                        t.wait_ge(S["e_sei"].h, 2 * e + 1)
                    for dc in range(ND):
                        tp = t.transpose(
                            psTb[:, dc * 128:(dc + 1) * 128],
                            ei_sm[:, ch, dc * 128:(dc + 1) * 128], idt_b[:, :])
                    if ch == 1:
                        t.wait_ge(S["e_b1"].h, 16 * (e + 1))
                        tp = t.transpose(
                            psTf[:, 0:32],
                            b1raw[0:32, (e % 2) * 128:(e % 2 + 1) * 128],
                            idt_f[0:32, 0:32])
                    S["e_tei"].inc(tp, 1)
                if e_stage < 3:
                    continue
                t.wait_ge(S["e_sei"].h, 2 * (e + 1))
                t.wait_ge(S["e_w1"].h, 16 * (e + 1))
                for o in range(8):
                    psH = psA if o % 2 == 0 else psB
                    if o == 0 and e >= 1:
                        t.wait_ge(S["e_veo"].h, 2 * e)
                    if 8 * e + o >= 2:
                        t.wait_ge(S["e_sh"].h, 8 * e + o - 1)
                    for fl in range(4):
                        fc = o * 4 + fl
                        dst = psH[:, fl * 256:(fl + 1) * 256]
                        for dc in range(ND):
                            mm = t.matmul(dst,
                                          a3[:, dc, fc * 128:(fc + 1) * 128],
                                          stgE[:, dc, :], start=(dc == 0),
                                          stop=(dc == 7))
                    S["e_thq"].inc(mm, 1)
                S["e_th"].inc(t.nop(), 1)
                if e_stage < 4:
                    continue
                t.wait_ge(S["e_sh"].h, 8 * (e + 1))
                t.wait_ge(S["e_w2"].h, 16 * (e + 1))
                t.wait_ge(S["e_b2"].h, 16 * (e + 1))
                for ch in range(2):
                    psE = psPV if ch == 0 else psA
                    if e >= 1:
                        t.wait_ge(S["e_veo"].h, 2 * e - 1 + ch)
                    for fc in range(32):
                        lt_ = sbigE[:, fc, ch * 128:(ch + 1) * 128]
                        t.matmul(psE[:, 0:512], lt_, b3[:, fc, 0:512],
                                 start=(fc == 0), stop=False)
                        t.matmul(psE[:, 512:1024], lt_, b3[:, fc, 512:1024],
                                 start=(fc == 0), stop=False)
                    t.matmul(psE[:, 0:512], onesl[e % 2][:, :],
                             b2sl[e % 2][:, 0:512], start=False, stop=True)
                    mm = t.matmul(psE[:, 512:1024], onesl[e % 2][:, :],
                                  b2sl[e % 2][:, 512:1024], start=False,
                                  stop=True)
                    S["e_teo"].inc(mm, 1)

        @blk.scalar
        def _(s):
            s.wait_ge(S["c_tg"].h, NT)
            for e in range(NEL):
                if e_stage < 2:
                    break
                for ch in range(2):
                    s.wait_ge(S["e_tei"].h, 2 * e + ch + 1)
                    ee = s.copy(
                        stgE[:, 0:8, ch * 128:(ch + 1) * 128],
                        psTb[:, 0:1024].rearrange("p (a b) -> p a b", b=128))
                    if ch == 1:
                        ee = s.copy(b1c[:, e % 2, :], psTf[:, 0:32])
                        sgap(s)
                    S["e_sei"].inc(ee, 1)
                for o in range(8):
                    if e_stage < 3:
                        break
                    psH = psA if o % 2 == 0 else psB
                    s.wait_ge(S["e_thq"].h, 8 * e + o + 1)
                    for fl in range(4):
                        fc = o * 4 + fl
                        srcp = psH[:, fl * 256:(fl + 1) * 256]
                        ge = s.activation(sbigE[:, fc, :], srcp,
                                          AF.Gelu_apprx_tanh,
                                          bias=b1c[:, e % 2, fc:fc + 1])
                    S["e_sh"].inc(ge, 1)

        @blk.vector
        def _(v):
            for e in range(NEL):
                if e_stage < 4:
                    break
                for ch in range(2):
                    psE = psPV if ch == 0 else psA
                    v.wait_ge(S["e_teo"].h, 2 * e + ch + 1)
                    if e >= 1:
                        v.wait_ge(S["e_sc"].h, 16 * (2 * (e - 1) + ch + 1))
                    gcol = gat_c[:, ch * 16 + e:ch * 16 + e + 1]
                    v.tensor_scalar(sb_f32[:, ch * 1024:ch * 1024 + 512],
                                    psE[:, 0:512], gcol, None, ALU.mult)
                    ve = v.tensor_scalar(
                        sb_f32[:, ch * 1024 + 512:(ch + 1) * 1024],
                        psE[:, 512:1024], gcol, None, ALU.mult)
                    S["e_veo"].inc(ve, 1)

    if dbg:
        with nc.Block() as blk:
            edd = S["d_dma"].v

            @blk.sync
            def _(sy):
                sy.wait_ge(S["e_sc"].h, S["e_sc"].v)
                S["d_dma"].inc(sy.dma_start(eidbg_h[:, :], stg[:, 1024:3072]), 16)
                S["d_dma"].inc(sy.dma_start(efdbg_h[:, :], eif_t[:, :]), 16)
                S["d_dma"].inc(sy.dma_start(hdbg_h[:, :], sbig[:, :]), 16)
                S["d_dma"].inc(sy.dma_start(eodbg_h[:, :], sb_f32[:, 0:2048]), 16)
                S["d_dma"].inc(sy.dma_start(w2dbg_h[:, :], ar_b[:, :]), 16)
                S["d_dma"].inc(sy.dma_start(b2dbg_h[:, :], wq_s[32:65, 0:1024]), 16)
                sy.wait_ge(S["d_dma"].h, edd + 96)

    if upto < 4:
        ctx.close()
        nc.compile()
        return nc
    with nc.Block() as blk:

        @blk.sync
        def _(sy):
            aba_h = [sb_f32[:, 0:1024], em[:, 512:1536]]
            oba_h = [sb_f32[:, 1024:2048], em[:, 1536:2560]]
            sy.wait_ge(S["e_sc"].h, S["e_sc"].v)
            for i in range(NT):
                if i >= 2:
                    sy.wait_ge(S["f_v"].h, i - 1)
                S["f_in"].inc(sy.dma_start(
                    aba_h[i % 2], acc_h[i * 128:(i + 1) * 128, :]), 16)
                S["f_in"].inc(sy.dma_start(
                    stg[:, (i % 2) * 1024:(i % 2 + 1) * 1024],
                    x2_h[i * 128:(i + 1) * 128, :]), 16)
                sy.wait_ge(S["f_v"].h, i + 1)
                S["f_out"].inc(sy.dma_start(
                    out_h[i * 128:(i + 1) * 128, :], oba_h[i % 2]), 16)
            sy.wait_ge(S["f_out"].h, 16 * NT)

        @blk.vector
        def _(v):
            aba_h = [sb_f32[:, 0:1024], em[:, 512:1536]]
            oba_h = [sb_f32[:, 1024:2048], em[:, 1536:2560]]
            for i in range(NT):
                ab = aba_h[i % 2]
                ob = oba_h[i % 2]
                xb = stg[:, (i % 2) * 1024:(i % 2 + 1) * 1024]
                v.wait_ge(S["f_in"].h, 32 * (i + 1))
                if i >= 2:
                    v.wait_ge(S["f_out"].h, 16 * (i - 1))
                S["f_vs"].inc(bn_var(v, ab), 1)
                v.wait_ge(S["f_ss"].h, i + 1)
                v.reciprocal(mst[:, 5:6], mst[:, 4:5])
                vgap(v)
                v.tensor_scalar(ob, ab, mst[:, 1:2], mst[:, 5:6],
                                ALU.subtract, ALU.mult)
                vgap(v)
                fo = v.tensor_tensor(ob, ob, xb, ALU.add)
                S["f_v"].inc(fo, 1)

        @blk.scalar
        def _(s):
            for i in range(NT):
                s.wait_ge(S["f_vs"].h, i + 1)
                st = s.activation(mst[:, 4:5], mst[:, 2:3], AF.Sqrt, bias=epsb[:, :])
                S["f_ss"].inc(st, 1)

    ctx.close()
    nc.compile()
    return nc


def kernel(L0_attn_X, L0_attn_W_Q, L0_attn_W_K, L0_attn_W_V, L0_ffn_gate_query,
           L0_ffn_log_temp, L0_ffn_w1, L0_ffn_b1, L0_ffn_w2, L0_ffn_b2):
    nc = build_nc()
    base = {
        "wq": np.ascontiguousarray(L0_attn_W_Q, dtype=np.float32),
        "wk": np.ascontiguousarray(L0_attn_W_K, dtype=np.float32),
        "wv": np.ascontiguousarray(L0_attn_W_V, dtype=np.float32),
        "gq": np.ascontiguousarray(L0_ffn_gate_query, dtype=np.float32),
        "ltemp": np.asarray(L0_ffn_log_temp, dtype=np.float32).reshape(1, 1),
        "w1": np.ascontiguousarray(L0_ffn_w1, dtype=np.float32),
        "b1": np.ascontiguousarray(L0_ffn_b1, dtype=np.float32),
        "w2": np.ascontiguousarray(L0_ffn_w2, dtype=np.float32),
        "b2": np.ascontiguousarray(L0_ffn_b2, dtype=np.float32),
    }
    in_maps = []
    for b in range(8):
        m = dict(base)
        m["x"] = np.ascontiguousarray(L0_attn_X[b], dtype=np.float32)
        in_maps.append(m)
    res = run_bass_kernel_spmd(nc, in_maps, core_ids=list(range(8)))
    return np.stack([r["out"] for r in res.results], axis=0)


# version 05
# speedup vs baseline: 1.0710x; 1.0710x over previous
from contextlib import ExitStack

import numpy as np

import concourse.bacc as bacc
import concourse.bass as bass
import concourse.mybir as mybir
from concourse.bass import IndirectOffsetOnAxis
from concourse.bass_utils import run_bass_kernel_spmd

F32 = mybir.dt.float32
BF16 = mybir.dt.bfloat16
I32 = mybir.dt.int32
I16 = mybir.dt.int16
AF = mybir.ActivationFunctionType
ALU = mybir.AluOpType
AX = mybir.AxisListType

T = 4096
D = 1024
NT = 32
ND = 8
NE = 16
DFF = 4096
CAP = 256
NCH = 4
EPS = 1e-5
SCALE = 1.0 / 32.0
BIS = 38
SINK = 6


class Sem:
    def __init__(self, ctx, nc, name):
        self.h = ctx.enter_context(nc.semaphore(name))
        self.v = 0

    def inc(self, instr, n=1):
        instr.then_inc(self.h, n)
        self.v += n
        return self.v


def build_nc(dbg=False, upto=5, ne_lim=NE, e_stage=9):
    nc = bacc.Bacc("TRN2")
    ctx = ExitStack()
    Et = ctx.enter_context

    x_h = nc.dram_tensor("x", [T, D], F32, kind="ExternalInput")
    wq_h = nc.dram_tensor("wq", [D, D], F32, kind="ExternalInput")
    wk_h = nc.dram_tensor("wk", [D, D], F32, kind="ExternalInput")
    wv_h = nc.dram_tensor("wv", [D, D], F32, kind="ExternalInput")
    gq_h = nc.dram_tensor("gq", [NE, D], F32, kind="ExternalInput")
    lt_h = nc.dram_tensor("ltemp", [1, 1], F32, kind="ExternalInput")
    w1_h = nc.dram_tensor("w1", [NE, D, DFF], F32, kind="ExternalInput")
    b1_h = nc.dram_tensor("b1", [NE, DFF], F32, kind="ExternalInput")
    w2_h = nc.dram_tensor("w2", [NE, DFF, D], F32, kind="ExternalInput")
    b2_h = nc.dram_tensor("b2", [NE, D], F32, kind="ExternalInput")
    out_h = nc.dram_tensor("out", [T, D], F32, kind="ExternalOutput")
    ikind = "ExternalOutput" if dbg else "Internal"
    x2_h = nc.dram_tensor("x2h", [T, D], BF16, kind=ikind)
    ln3_h = nc.dram_tensor("ln3h", [T, D], BF16, kind=ikind)
    acc_h = nc.dram_tensor("acch", [T, D], F32, kind=ikind)
    adram = nc.dram_tensor("adram", [16, 4096], F32, kind="Internal")
    if dbg:
        gdbg_h = nc.dram_tensor("gdbg", [NE, T], F32, kind="ExternalOutput")
        kdbg_h = nc.dram_tensor("kdbg", [128, ND * T], BF16, kind="ExternalOutput")
        vdbg_h = nc.dram_tensor("vdbg", [128, NT * D], BF16, kind="ExternalOutput")
        adbg_h = nc.dram_tensor("adbg", [128, 512], F32, kind="ExternalOutput")
        idbg_h = nc.dram_tensor("idbg", [128, 2 * NE], F32, kind="ExternalOutput")
        wdbg_h = nc.dram_tensor("wdbg", [128, 2 * NE], F32, kind="ExternalOutput")
        eidbg_h = nc.dram_tensor("eidbg", [128, 2048], BF16, kind="ExternalOutput")
        efdbg_h = nc.dram_tensor("efdbg", [128, 2048], BF16, kind="ExternalOutput")
        hdbg_h = nc.dram_tensor("hdbg", [128, 8192], BF16, kind="ExternalOutput")
        eodbg_h = nc.dram_tensor("eodbg", [128, 2048], F32, kind="ExternalOutput")
        w2dbg_h = nc.dram_tensor("w2dbg", [128, NT * D], BF16, kind="ExternalOutput")
        b2dbg_h = nc.dram_tensor("b2dbg", [33, 1024], BF16, kind="ExternalOutput")

    ar_a = Et(nc.sbuf_tensor("ar_a", [128, ND * T], BF16))
    ar_b = Et(nc.sbuf_tensor("ar_b", [128, NT * D], BF16))
    a3 = ar_a[:, :].rearrange("p (a b) -> p a b", b=T)
    b3 = ar_b[:, :].rearrange("p (a b) -> p a b", b=D)

    sbig = Et(nc.sbuf_tensor("sbig", [128, 8192], BF16))
    sbigE = sbig[:, 0:8192].rearrange("p (a b) -> p a b", b=256)

    sb_f32 = Et(nc.sbuf_tensor("sbf32", [128, 2048], F32))
    stg = Et(nc.sbuf_tensor("stg", [128, 3072], BF16))

    wq_s = Et(nc.sbuf_tensor("wq_s", [128, ND * D], BF16))
    wk_s = Et(nc.sbuf_tensor("wk_s", [128, ND * D], BF16))
    wv_s = Et(nc.sbuf_tensor("wv_s", [128, ND * D], BF16))
    wq3 = wq_s[:, :].rearrange("p (a b) -> p a b", b=D)
    wk3 = wk_s[:, :].rearrange("p (a b) -> p a b", b=D)
    wv3 = wv_s[:, :].rearrange("p (a b) -> p a b", b=D)
    gt = nc.alloc_sbuf_tensor_at("gt", [128, 4096], F32,
                                 offset=nc.lookup_mloc(wk_s).addr)
    em = nc.alloc_sbuf_tensor_at("em", [128, 4096], F32,
                                 offset=nc.lookup_mloc(wv_s).addr)

    gq_s = ar_a[0:16, 0:1024]
    gqT = Et(nc.sbuf_tensor("gqT", [128, 128], BF16))
    g_em = em[0:16, 0:4096]
    aemv = nc.alloc_sbuf_tensor_at("aemv", [16, 4096], F32,
                                   offset=nc.lookup_mloc(wq_s).addr)
    lt_s = Et(nc.sbuf_tensor("lt_s", [1, 1], F32))
    rt1 = Et(nc.sbuf_tensor("rt1", [1, 1], F32))
    rt16 = Et(nc.sbuf_tensor("rt16", [16, 1], F32))

    idt_b = Et(nc.sbuf_tensor("idt_b", [128, 128], BF16))
    idt_f = Et(nc.sbuf_tensor("idt_f", [128, 128], F32))
    iof = Et(nc.sbuf_tensor("iof", [128, 128], F32))
    iop = Et(nc.sbuf_tensor("iop", [128, 1], F32))

    ln1t = sbig[:, 7168:8192]
    mst = Et(nc.sbuf_tensor("mst", [128, 24], F32))
    hz = Et(nc.sbuf_tensor("hz", [1, 8], F32))
    epsb = Et(nc.sbuf_tensor("epsb", [128, 1], F32))

    pmod = Et(nc.sbuf_tensor("pmod", [128, 1], F32))
    pdiv = Et(nc.sbuf_tensor("pdiv", [128, 1], F32))
    ind_e = sb_f32[:, 1296:1312]
    carry_m = sb_f32[:, 1152:1280]
    lo_s = sb_f32[:, 1280:1281]
    hi_s = sb_f32[:, 1281:1282]
    mid_s = sb_f32[:, 1282:1283]
    cnt_p = sb_f32[:, 1283:1284]
    c16 = sb_f32[0:16, 1284:1285]
    cb_s = sb_f32[:, 1285:1286]
    i32v = nc.alloc_sbuf_tensor_at(
        "i32v", [128, 2], I32,
        offset=nc.lookup_mloc(sb_f32).addr + 1376 * 4)
    msk_s = i32v[:, 0:1]
    nmsk_s = i32v[:, 1:2]
    m16 = sb_f32[0:16, 1286:1287]
    nm16 = sb_f32[0:16, 1287:1288]
    rs16 = sb_f32[0:16, 1288:1289]
    seg32 = sb_f32[:, 1312:1344]
    vi16 = nc.alloc_sbuf_tensor_at("vi16", [128, 4096], I16,
                                   offset=nc.lookup_mloc(sbig).addr)
    tokid = vi16[:, 0:512]
    toff = sb_f32[:, 1289:1290]
    msi = vi16[:, 512:1024]
    a_bf = sbig[:, 3072:3584]
    dstA = vi16[:, 1024:1280]
    dstB = sbig[:, 3584:3840]
    idx_f = sb_f32[:, 1344:1376]
    gat_c = em[:, 320:352]
    idx_i = nc.alloc_sbuf_tensor_at(
        "idx_i", [128, 2 * NE], I32,
        offset=nc.lookup_mloc(wv_s).addr + 12288)

    ei_sm = stg[:, 1024:3072].rearrange("p (a b) -> p a b", b=D)
    eif_t = nc.alloc_sbuf_tensor_at(
        "eif_t", [128, 2048], BF16, offset=nc.lookup_mloc(wv_s).addr + 8192)
    stgE = eif_t[:, :].rearrange("p (a b) -> p a b", b=256)
    b1c = em[:, 256:320].rearrange("p (a b) -> p a b", b=32)
    b1raw = em[0:32, 0:256]
    b2sl = [wq_s[32:33, 0:1024], wq_s[64:65, 0:1024]]
    onesl = [wq_s[32:33, 1024:1152], wq_s[64:65, 1024:1152]]

    psA = Et(nc.psum_tensor("psA", [128, 1024], F32))
    psB = Et(nc.psum_tensor("psB", [128, 1024], F32))
    psPV = Et(nc.psum_tensor("psPV", [128, 1024], F32))
    psTf = Et(nc.psum_tensor("psTf", [128, 512], F32))
    psTb = Et(nc.psum_tensor("psTb", [128, 1024], BF16))

    names = [
        "a_id", "a_zm", "a_w", "a_gq", "a_tgq", "a_gqe", "a_x", "a_vs", "a_ss",
        "a_vln", "a_t1", "a_ttp", "a_se", "a_tk", "a_sk", "a_tv", "a_svv", "a_z",
        "c_x", "c_ep", "c_t1", "c_vln", "c_vs", "c_ss", "c_se1", "c_tq", "c_sq",
        "c_ts", "c_vmax", "c_sexp", "c_tpt", "c_spt", "c_tpv", "c_vpv",
        "c_st", "c_t3", "c_se3", "c_tg", "c_vg",
        "d_t", "d_s", "d_v", "d_g", "d_dma",
        "e_w1", "e_w2", "e_b1", "e_b2", "e_gth", "e_tei", "e_sei", "e_thq",
        "e_sh", "e_th", "e_teo", "e_veo", "e_sc",
        "f_in", "f_vs", "f_ss", "f_v", "f_out", "hv", "hs",
    ]
    S = {n: Sem(ctx, nc, n) for n in names}

    xbuf = sb_f32[:, 0:1024]
    O_sb = sb_f32[:, 1024:2048]
    xb_b = gt[:, 0:1024]
    xba = [xbuf, xb_b]
    cdmp = gt[:, 1024:2048]

    def lnfmA(r):
        return sbig[:, 0:4096].rearrange("p (a b) -> p a b", b=512)

    P_sl = [sbig[:, 0:1024], sbig[:, 1024:2048]]
    Pt_sl = [sbig[:, 2048:3072], sbig[:, 3072:4096]]
    lnfm_flat = sbig[:, 4096:5120]
    lnfmC = lnfm_flat.rearrange("p (a b) -> p a b", b=128)
    Q_sl = sbig[:, 5120:6144]
    ln3_flat = sbig[:, 6144:7168]
    ln3fm = ln3_flat.rearrange("p (a b) -> p a b", b=128)

    stg_x2 = [stg[:, 0:1024], stg[:, 1024:2048]]
    stg_l3 = stg[:, 2048:3072]

    T_em = em[0:16, 0:4096]
    A_em = aemv[:, :]
    A_r = gt[:, 0:512]
    sel_g = gt[:, 512:1024]
    slots_g = gt[:, 1024:1536]
    msf_g = gt[:, 1536:2048]
    zer_g = gt[:, 2048:2560]
    dmp_g = gt[:, 2560:3072]
    tmA = gt[:, 3072:3584]
    tmB = gt[:, 3584:4096]
    dstAf = gt[:, 2560:2816]
    dstBf = gt[:, 2816:3072]
    idx_em = gt[0:16, 2048:2304]
    gat_em = gt[0:16, 2304:2560]
    fm128 = gt[:, 3072:3200]
    fd128 = gt[:, 3200:3328]
    onesf = gt[0:1, 3328:3456]
    ind_et = sb_f32[0:16, 1024:1152]

    def vgap(v):
        dmy = v.tensor_copy(hz[0:1, 0:1], hz[0:1, 1:2])
        n = S["hv"].inc(dmy, 1)
        v.wait_ge(S["hv"].h, n)

    def sgap(s):
        dmy = s.copy(hz[0:1, 2:3], hz[0:1, 3:4])
        n = S["hs"].inc(dmy, 1)
        s.wait_ge(S["hs"].h, n)

    def ln_stats_v(v, src, dump, vs_sem, ss_sem, ss_val):
        v.reduce_sum(mst[:, 0:1], src, AX.X)
        v.scalar_tensor_tensor(dump, src, 1.0, src, ALU.mult, ALU.mult,
                               accum_out=mst[:, 2:3])
        vgap(v)
        v.tensor_scalar(mst[:, 1:2], mst[:, 0:1], 1.0 / D, None, ALU.mult)
        v.tensor_scalar(mst[:, 3:4], mst[:, 2:3], 1.0 / D, None, ALU.mult)
        vgap(v)
        v.tensor_tensor(mst[:, 11:12], mst[:, 1:2], mst[:, 1:2], ALU.mult)
        vgap(v)
        vr = v.tensor_tensor(mst[:, 3:4], mst[:, 3:4], mst[:, 11:12],
                             ALU.subtract)
        vs_sem.inc(vr, 1)
        v.wait_ge(ss_sem.h, ss_val)
        v.reciprocal(mst[:, 5:6], mst[:, 4:5])
        vgap(v)

    bnsc = iof[:, 16:28]

    def bn_var(v, src):
        v.bn_stats(bnsc[:, 0:6], src[:, 0:512])
        v.bn_stats(bnsc[:, 6:12], src[:, 512:1024])
        vgap(v)
        return v.bn_aggr(mst[:, 1:3], bnsc[:, 0:12])

    with nc.Block() as blk:

        @blk.gpsimd
        def _(g):
            g.iota(iof[:, :], [[1, 128]], channel_multiplier=0,
                   allow_small_or_imprecise_dtypes=True)
            it = g.iota(iop[:, :], [[0, 1]], channel_multiplier=1,
                        allow_small_or_imprecise_dtypes=True)
            S["a_id"].inc(it, 1)
            S["a_w"].inc(g.dma_start(
                wk3[:, :, :], wk_h[:, :].rearrange("(a p) e -> p a e", p=128)), 16)
            S["a_w"].inc(g.dma_start(
                wv3[:, :, :], wv_h[:, :].rearrange("(a p) e -> p a e", p=128)), 16)
            S["a_w"].inc(g.dma_start(
                wq3[:, :, :], wq_h[:, :].rearrange("(a p) e -> p a e", p=128)), 16)
            S["a_gq"].inc(g.dma_start(gq_s, gq_h[:, :]), 16)
            g.wait_ge(S["a_zm"].h, 1)
            for i in range(NT):
                S["a_z"].inc(g.dma_start(
                    acc_h[i * 128:(i + 1) * 128, :], stg[:, 0:1024]), 16)

        @blk.sync
        def _(sy):
            S["a_gq"].inc(sy.dma_start(lt_s[:, :], lt_h[:, :]), 16)
            for i in range(NT):
                if i >= 1:
                    sy.wait_ge(S["a_vln"].h, i)
                S["a_x"].inc(sy.dma_start(
                    xbuf, x_h[i * 128:(i + 1) * 128, :]), 16)

        @blk.vector
        def _(v):
            v.wait_ge(S["a_id"].h, 1)
            v.tensor_scalar(idt_b[:, :], iof[:, 0:128], iop[:, :], None,
                            ALU.is_equal)
            v.tensor_scalar(idt_f[:, :], iof[:, 0:128], iop[:, :], None,
                            ALU.is_equal)
            v.memset(epsb[:, :], EPS)
            zm = v.memset(stg[:, 0:1024], 0.0)
            S["a_zm"].inc(zm, 1)
            for i in range(NT):
                xb = xbuf
                v.wait_ge(S["a_x"].h, 16 * (i + 1))
                S["a_vs"].inc(bn_var(v, xb), 1)
                v.wait_ge(S["a_ss"].h, i + 1)
                v.reciprocal(mst[:, 5:6], mst[:, 4:5])
                vgap(v)
                if i >= 1:
                    v.wait_ge(S["a_t1"].h, i)
                ap = v.tensor_scalar(ln1t[:, :], xb, mst[:, 1:2], mst[:, 5:6],
                                     ALU.subtract, ALU.mult)
                S["a_vln"].inc(ap, 1)

        @blk.scalar
        def _(s):
            s.wait_ge(S["a_tgq"].h, 1)
            ge = s.copy(gqT[:, :], psTb[:, 0:128])
            S["a_gqe"].inc(ge, 1)
            for i in range(NT):
                s.wait_ge(S["a_vs"].h, i + 1)
                st = s.activation(mst[:, 4:5], mst[:, 2:3], AF.Sqrt, bias=epsb[:, :])
                S["a_ss"].inc(st, 1)
                s.wait_ge(S["a_ttp"].h, i + 1)
                r, tt = i // 4, i % 4
                dst = lnfmA(r)
                s.copy(dst[:, 0:4, tt * 128:(tt + 1) * 128],
                       psTb[:, 0:512].rearrange("p (a b) -> p a b", b=128))
                se = s.copy(dst[:, 4:8, tt * 128:(tt + 1) * 128],
                            psTb[:, 512:1024].rearrange("p (a b) -> p a b", b=128))
                S["a_se"].inc(se, 1)
                if tt == 3:
                    for ec in range(8):
                        k = r * 8 + ec
                        s.wait_ge(S["a_tk"].h, k + 1)
                        ke = s.copy(a3[:, ec, r * 512:(r + 1) * 512],
                                    psA[:, (ec % 2) * 512:(ec % 2 + 1) * 512])
                        S["a_sk"].inc(ke, 1)
                    for t2 in range(4):
                        gg = r * 4 + t2
                        s.wait_ge(S["a_tv"].h, gg + 1)
                        ve = s.copy(b3[:, gg, :], psPV[:, :])
                        S["a_svv"].inc(ve, 1)

        @blk.tensor
        def _(t):
            t.wait_ge(S["a_id"].h, 1)
            t.wait_ge(S["a_gq"].h, 32)
            for dc in range(ND):
                tp = t.transpose(psTb[:, dc * 16:(dc + 1) * 16],
                                 gq_s[0:16, dc * 128:(dc + 1) * 128],
                                 idt_b[0:16, 0:16])
            S["a_tgq"].inc(tp, 1)
            t.wait_ge(S["a_w"].h, 32)
            for i in range(NT):
                r, tt = i // 4, i % 4
                t.wait_ge(S["a_vln"].h, i + 1)
                if i == 0:
                    t.wait_ge(S["a_gqe"].h, 1)
                else:
                    t.wait_ge(S["a_se"].h, i)
                for dc in range(ND):
                    tp = t.transpose(psTb[:, dc * 128:(dc + 1) * 128],
                                     ln1t[:, dc * 128:(dc + 1) * 128],
                                     idt_b[:, :])
                S["a_ttp"].inc(tp, 1)
                S["a_t1"].inc(t.nop(), 1)
                if tt == 3:
                    lf = lnfmA(r)
                    t.wait_ge(S["a_se"].h, 4 * (r + 1))
                    for ec in range(8):
                        k = r * 8 + ec
                        if k >= 2:
                            t.wait_ge(S["a_sk"].h, k - 1)
                        dsts = psA[:, (ec % 2) * 512:(ec % 2 + 1) * 512]
                        for dc in range(ND):
                            mm = t.matmul(dsts,
                                          wk3[:, dc, ec * 128:(ec + 1) * 128],
                                          lf[:, dc, :], start=(dc == 0),
                                          stop=(dc == 7))
                        S["a_tk"].inc(mm, 1)
                    for t2 in range(4):
                        gg = r * 4 + t2
                        if gg >= 1:
                            t.wait_ge(S["a_svv"].h, gg)
                        for dc in range(ND):
                            t.matmul(psPV[:, 0:512],
                                     lf[:, dc, t2 * 128:(t2 + 1) * 128],
                                     wv3[:, dc, 0:512],
                                     start=(dc == 0), stop=(dc == 7))
                            mm = t.matmul(psPV[:, 512:1024],
                                          lf[:, dc, t2 * 128:(t2 + 1) * 128],
                                          wv3[:, dc, 512:1024],
                                          start=(dc == 0), stop=(dc == 7))
                        S["a_tv"].inc(mm, 1)

    if upto < 1:
        if dbg:
            wdbg2 = nc.dram_tensor("wdbg2", [128, ND * D], BF16,
                                   kind="ExternalOutput")
            ldbg = nc.dram_tensor("ldbg", [128, D], BF16, kind="ExternalOutput")
            mdbg = nc.dram_tensor("mdbg", [128, 16], F32, kind="ExternalOutput")
            with nc.Block() as blk:
                @blk.sync
                def _(sy):
                    sy.wait_ge(S["a_vln"].h, NT)
                    sy.dma_start(wdbg2[:, :], wk_s[:, :]).then_inc(S["d_dma"].h, 16)
                    sy.dma_start(ldbg[:, :], ln1t).then_inc(S["d_dma"].h, 16)
                    sy.dma_start(mdbg[:, :], mst[:, 0:16]).then_inc(S["d_dma"].h, 16)
                    sy.wait_ge(S["d_dma"].h, S["d_dma"].v + 48)
            S["d_dma"].v += 48
        ctx.close()
        nc.compile()
        return nc
    with nc.Block() as blk:

        @blk.sync
        def _(sy):
            if dbg:
                sy.wait_ge(S["a_sk"].h, 64)
                sy.wait_ge(S["a_svv"].h, 32)
                sy.dma_start(kdbg_h[:, :], ar_a[:, :]).then_inc(S["d_dma"].h, 16)
                sy.dma_start(vdbg_h[:, :], ar_b[:, :]).then_inc(S["d_dma"].h, 16)
                sy.wait_ge(S["d_dma"].h, 32)
                S["d_dma"].v += 32
            sy.wait_ge(S["a_vln"].h, 32)
            S["c_x"].inc(sy.dma_start(xba[0], x_h[0:128, :]), 16)
            sy.wait_ge(S["a_tk"].h, 64)
            S["c_x"].inc(sy.dma_start(xba[1], x_h[128:256, :]), 16)
            for i in range(NT):
                sy.wait_ge(S["c_ep"].h, i + 1)
                if i + 2 < NT:
                    S["c_x"].inc(sy.dma_start(
                        xba[i % 2], x_h[(i + 2) * 128:(i + 3) * 128, :]), 16)
                S["c_st"].inc(sy.dma_start(
                    x2_h[i * 128:(i + 1) * 128, :], stg_x2[i % 2]), 16)
                S["c_st"].inc(sy.dma_start(
                    ln3_h[i * 128:(i + 1) * 128, :], stg_l3), 16)

        @blk.vector
        def _(v):
            v.wait_ge(S["c_x"].h, 16)
            S["c_vs"].inc(bn_var(v, xbuf), 1)
            v.wait_ge(S["c_ss"].h, 1)
            v.reciprocal(mst[:, 5:6], mst[:, 4:5])
            vgap(v)
            v.wait_ge(S["a_t1"].h, NT)
            ap = v.tensor_scalar(ln1t[:, :], xbuf, mst[:, 1:2], mst[:, 5:6],
                                 ALU.subtract, ALU.mult)
            S["c_vln"].inc(ap, 1)
            for i in range(NT):
                xb = xba[i % 2]
                for c in range(NCH):
                    ps = psA if c % 2 == 0 else psB
                    v.wait_ge(S["c_ts"].h, 4 * i + c + 1)
                    v.reduce_max(mst[:, 11:12], ps[:, 0:1024], AX.X)
                    vgap(v)
                    if c == 0:
                        v.tensor_copy(mst[:, 6:7], mst[:, 11:12])
                        vgap(v)
                        bx = v.tensor_scalar(mst[:, 10:11], mst[:, 6:7],
                                             -SCALE, None, ALU.mult)
                    else:
                        v.tensor_tensor(mst[:, 8:9], mst[:, 11:12],
                                        mst[:, 6:7], ALU.max)
                        vgap(v)
                        v.tensor_tensor(mst[:, 9:10], mst[:, 6:7], mst[:, 8:9],
                                        ALU.subtract)
                        v.tensor_scalar(mst[:, 10:11], mst[:, 8:9], -SCALE,
                                        None, ALU.mult)
                        v.tensor_copy(mst[:, 6:7], mst[:, 8:9])
                        vgap(v)
                        bx = v.tensor_scalar(mst[:, 9:10], mst[:, 9:10],
                                             SCALE, None, ALU.mult)
                    S["c_vmax"].inc(bx, 1)
                    if c == 3 and i + 1 < NT:
                        v.wait_ge(S["c_x"].h, 16 * (i + 2))
                        S["c_vs"].inc(bn_var(v, xba[(i + 1) % 2]), 1)
                    v.wait_ge(S["c_sexp"].h, 4 * i + c + 1)
                    if c == 0:
                        v.tensor_copy(mst[:, 7:8], mst[:, 13:14])
                    else:
                        v.scalar_tensor_tensor(mst[:, 7:8], mst[:, 7:8],
                                               mst[:, 9:10], mst[:, 13:14],
                                               ALU.mult, ALU.add)
                    if c == 3 and i + 1 < NT:
                        v.wait_ge(S["c_ss"].h, 3 * i + 2)
                        v.reciprocal(mst[:, 5:6], mst[:, 4:5])
                        vgap(v)
                        v.wait_ge(S["c_t1"].h, i + 1)
                        ap = v.tensor_scalar(ln1t[:, :], xba[(i + 1) % 2],
                                             mst[:, 1:2], mst[:, 5:6],
                                             ALU.subtract, ALU.mult)
                        S["c_vln"].inc(ap, 1)
                    v.wait_ge(S["c_tpv"].h, 4 * i + c + 1)
                    if c == 0:
                        pv = v.tensor_copy(O_sb[:, 0:1024], psPV[:, 0:1024])
                    else:
                        pv = v.scalar_tensor_tensor(
                            O_sb[:, 0:1024], O_sb[:, 0:1024], mst[:, 9:10],
                            psPV[:, 0:1024], ALU.mult, ALU.add)
                    S["c_vpv"].inc(pv, 1)
                p2 = 3 * i + 3 if i + 1 < NT else 3 * i + 2
                v.reciprocal(mst[:, 15:16], mst[:, 7:8])
                vgap(v)
                v.tensor_tensor(mst[:, 22:23], mst[:, 15:16], mst[:, 15:16],
                                ALU.mult)
                bn_var(v, O_sb)
                vgap(v)
                vn = v.tensor_tensor(mst[:, 2:3], mst[:, 2:3], mst[:, 22:23],
                                     ALU.mult)
                S["c_vs"].inc(vn, 1)
                v.wait_ge(S["c_ss"].h, p2)
                v.reciprocal(mst[:, 5:6], mst[:, 4:5])
                vgap(v)
                v.tensor_scalar(O_sb, O_sb, mst[:, 1:2], mst[:, 5:6],
                                ALU.subtract, ALU.mult)
                vgap(v)
                v.scalar_tensor_tensor(xb, O_sb, mst[:, 15:16], xb,
                                       ALU.mult, ALU.add)
                vgap(v)
                S["c_vs"].inc(bn_var(v, xb), 1)
                v.wait_ge(S["c_ss"].h, p2 + 1)
                v.reciprocal(mst[:, 5:6], mst[:, 4:5])
                vgap(v)
                if i < 2:
                    v.wait_ge(S["a_z"].h, 16 * NT)
                if i >= 1:
                    v.wait_ge(S["c_st"].h, 32 * i)
                    v.wait_ge(S["c_t3"].h, i)
                v.tensor_copy(stg_x2[i % 2], xb)
                ep = v.tensor_scalar(stg_l3, xb, mst[:, 1:2], mst[:, 5:6],
                                     ALU.subtract, ALU.mult)
                S["c_ep"].inc(ep, 1)

        @blk.scalar
        def _(s):

            def sqrt_s(val):
                s.wait_ge(S["c_vs"].h, val)
                r = s.activation(mst[:, 4:5], mst[:, 2:3], AF.Sqrt,
                                 bias=epsb[:, :])
                S["c_ss"].inc(r, 1)

            for i in range(NT):
                if i == 0:
                    sqrt_s(1)
                    s.preload_act_table(AF.Exp)
                s.wait_ge(S["c_t1"].h, i + 1)
                le = s.copy(lnfm_flat, psTb[:, 0:1024])
                S["c_se1"].inc(le, 1)
                for ec in range(8):
                    s.wait_ge(S["c_tq"].h, 8 * i + ec + 1)
                    qe = s.copy(Q_sl[:, ec * 128:(ec + 1) * 128],
                                psPV[:, (ec % 2) * 512:(ec % 2) * 512 + 128])
                    S["c_sq"].inc(qe, 1)
                if i >= 1:
                    sqrt_s(3 * i)
                    sqrt_s(3 * i + 1)
                    s.preload_act_table(AF.Exp)
                for c in range(NCH):
                    ps = psA if c % 2 == 0 else psB
                    s.wait_ge(S["c_vmax"].h, 4 * i + c + 1)
                    if c > 0:
                        s.activation(mst[:, 9:10], mst[:, 9:10], AF.Exp)
                    ex = s.activation(P_sl[c % 2][:, 0:1024], ps[:, 0:1024],
                                      AF.Exp, bias=mst[:, 10:11], scale=SCALE,
                                      accum_out=mst[:, 13:14])
                    S["c_sexp"].inc(ex, 1)
                    s.wait_ge(S["c_tpt"].h, 4 * i + c + 1)
                    pe = s.copy(Pt_sl[c % 2], psTb[:, 0:1024])
                    S["c_spt"].inc(pe, 1)
                s.preload_act_table(AF.Sqrt)
                if i + 1 < NT:
                    sqrt_s(3 * i + 2)
                if i >= 1:
                    s.wait_ge(S["c_t3"].h, i)
                    l3 = s.copy(ln3_flat, psTb[:, 0:1024])
                    S["c_se3"].inc(l3, 1)
                if i >= 2:
                    s.wait_ge(S["c_tg"].h, i - 1)
                    gv = s.copy(g_em[:, (i - 2) * 128:(i - 1) * 128],
                                psTf[0:16, 0:128])
                    S["c_vg"].inc(gv, 1)
            sqrt_s(3 * NT - 1)
            sqrt_s(3 * NT)
            s.wait_ge(S["c_t3"].h, NT)
            l3 = s.copy(ln3_flat, psTb[:, 0:1024])
            S["c_se3"].inc(l3, 1)
            s.wait_ge(S["c_tg"].h, NT - 1)
            gv = s.copy(g_em[:, (NT - 2) * 128:(NT - 1) * 128],
                        psTf[0:16, 0:128])
            S["c_vg"].inc(gv, 1)
            s.wait_ge(S["c_tg"].h, NT)
            gv = s.copy(g_em[:, (NT - 1) * 128:NT * 128], psTf[0:16, 0:128])
            S["c_vg"].inc(gv, 1)

        @blk.tensor
        def _(t):
            t.wait_ge(S["a_sk"].h, 64)
            t.wait_ge(S["a_svv"].h, 32)
            t.wait_ge(S["a_w"].h, 48)

            def g_mm(j):
                t.wait_ge(S["c_se3"].h, j + 1)
                t.wait_ge(S["a_gqe"].h, 1)
                if j >= 1:
                    t.wait_ge(S["c_vg"].h, j)
                for dc in range(ND):
                    gm = t.matmul(psTf[0:16, 0:128],
                                  gqT[:, dc * 16:(dc + 1) * 16],
                                  ln3fm[:, dc, :], start=(dc == 0),
                                  stop=(dc == 7))
                S["c_tg"].inc(gm, 1)

            for i in range(NT):
                t.wait_ge(S["c_vln"].h, i + 1)
                if i == 0:
                    t.wait_ge(S["a_se"].h, 32)
                elif i == 1:
                    t.wait_ge(S["c_spt"].h, 4)
                else:
                    t.wait_ge(S["c_se3"].h, i - 1)
                for dc in range(ND):
                    tp = t.transpose(psTb[:, dc * 128:(dc + 1) * 128],
                                     ln1t[:, dc * 128:(dc + 1) * 128],
                                     idt_b[:, :])
                S["c_t1"].inc(tp, 1)
                t.wait_ge(S["c_se1"].h, i + 1)
                if i >= 1:
                    t.wait_ge(S["c_vpv"].h, 4 * i)
                for ec in range(8):
                    if ec >= 2:
                        t.wait_ge(S["c_sq"].h, 8 * i + ec - 1)
                    dq = psPV[:, (ec % 2) * 512:(ec % 2) * 512 + 128]
                    for dc in range(ND):
                        mm = t.matmul(dq, wq3[:, dc, ec * 128:(ec + 1) * 128],
                                      lnfmC[:, dc, :], start=(dc == 0),
                                      stop=(dc == 7))
                    S["c_tq"].inc(mm, 1)

                def s_mm(c):
                    ps = psA if c % 2 == 0 else psB
                    if 4 * i + c >= 2:
                        t.wait_ge(S["c_sexp"].h, 4 * i + c - 1)
                    for ec in range(8):
                        lt_ = Q_sl[:, ec * 128:(ec + 1) * 128]
                        t.matmul(ps[:, 0:512], lt_,
                                 a3[:, ec, c * 1024:c * 1024 + 512],
                                 start=(ec == 0), stop=(ec == 7))
                        mm = t.matmul(ps[:, 512:1024], lt_,
                                      a3[:, ec, c * 1024 + 512:(c + 1) * 1024],
                                      start=(ec == 0), stop=(ec == 7))
                    S["c_ts"].inc(mm, 1)

                s_mm(0)
                s_mm(1)
                if i >= 2:
                    g_mm(i - 2)
                for c in range(NCH):
                    t.wait_ge(S["c_sexp"].h, 4 * i + c + 1)
                    if c > 0:
                        t.wait_ge(S["c_spt"].h, 4 * i + c)
                    for sub in range(8):
                        tp = t.transpose(psTb[:, sub * 128:(sub + 1) * 128],
                                         P_sl[c % 2][:, sub * 128:(sub + 1) * 128],
                                         idt_b[:, :])
                    S["c_tpt"].inc(tp, 1)
                    t.wait_ge(S["c_spt"].h, 4 * i + c + 1)
                    if c == 0:
                        t.wait_ge(S["c_sq"].h, 8 * (i + 1))
                    else:
                        t.wait_ge(S["c_vpv"].h, 4 * i + c)
                    for sub in range(8):
                        lt_ = Pt_sl[c % 2][:, sub * 128:(sub + 1) * 128]
                        t.matmul(psPV[:, 0:512], lt_, b3[:, c * 8 + sub, 0:512],
                                 start=(sub == 0), stop=(sub == 7))
                        mm = t.matmul(psPV[:, 512:1024], lt_,
                                      b3[:, c * 8 + sub, 512:1024],
                                      start=(sub == 0), stop=(sub == 7))
                    S["c_tpv"].inc(mm, 1)
                    if c + 2 < NCH:
                        s_mm(c + 2)
                if i >= 1:
                    t.wait_ge(S["c_ep"].h, i)
                    t.wait_ge(S["c_spt"].h, 4 * (i + 1))
                    for dc in range(ND):
                        tp = t.transpose(psTb[:, dc * 128:(dc + 1) * 128],
                                         stg_l3[:, dc * 128:(dc + 1) * 128],
                                         idt_b[:, :])
                    S["c_t3"].inc(tp, 1)
            t.wait_ge(S["c_ep"].h, NT)
            t.wait_ge(S["c_spt"].h, 4 * NT)
            for dc in range(ND):
                tp = t.transpose(psTb[:, dc * 128:(dc + 1) * 128],
                                 stg_l3[:, dc * 128:(dc + 1) * 128],
                                 idt_b[:, :])
            S["c_t3"].inc(tp, 1)
            g_mm(NT - 2)
            g_mm(NT - 1)

    if upto < 2:
        ctx.close()
        nc.compile()
        return nc
    with nc.Block() as blk:
        d0s = S["d_s"].v
        d0t = S["d_t"].v
        d0v = S["d_v"].v
        d0g = S["d_g"].v

        @blk.scalar
        def _(s):
            s.wait_ge(S["a_gq"].h, 16)
            rv = s.activation(rt1[:, :], lt_s[:, :], AF.Exp, scale=-1.0)
            S["d_s"].inc(rv, 1)

        @blk.tensor
        def _(t):
            t.wait_ge(S["d_s"].h, d0s + 1)
            t.wait_ge(S["d_v"].h, d0v + 1)
            t.wait_ge(S["c_sexp"].h, 4 * NT)
            t.wait_ge(S["c_se3"].h, NT)
            bm = t.matmul(psTf[0:16, 0:1], onesf[0:1, 0:16], rt1[:, :],
                          start=True, stop=True)
            S["d_t"].inc(bm, 1)
            t.wait_ge(S["d_g"].h, d0g + 1)
            t.transpose(psTf[:, 64:65], tmB[0:1, 0:128], idt_f[0:1, 0:1])
            tr = t.transpose(psTf[:, 65:66], tmB[0:1, 128:256], idt_f[0:1, 0:1])
            S["d_t"].inc(tr, 1)

        @blk.vector
        def _(v):
            v.wait_ge(S["c_vg"].h, NT)
            v.memset(onesf[:, :], 1.0)
            v.memset(onesl[0], 1.0)
            om = v.memset(onesl[1], 1.0)
            S["d_v"].inc(om, 1)
            v.wait_ge(S["d_t"].h, d0t + 2)
            v.wait_ge(S["d_g"].h, d0g + 1)
            v.tensor_copy(rt16[:, :], psTf[0:16, 0:1])
            v.tensor_copy(pmod[:, :], psTf[:, 64:65])
            v.tensor_copy(pdiv[:, :], psTf[:, 65:66])
            vgap(v)
            v.tensor_scalar(ind_e[:, :], iof[:, 0:16], pmod[:, :], None,
                            ALU.is_equal)
            v.tensor_scalar(carry_m[:, :], fm128[:, :], pmod[:, :], None,
                            ALU.is_equal)
            v.tensor_scalar(dmp_g[:, 0:128], fd128[:, :], pdiv[:, :], None,
                            ALU.is_gt)
            vgap(v)
            v.tensor_tensor(carry_m[:, :], carry_m[:, :], dmp_g[:, 0:128],
                            ALU.mult)
            v.memset(zer_g, 0.0)
            v.memset(lo_s[:, :], 0.0)
            v.memset(hi_s[:, :], 1.0)
            v.tensor_scalar(T_em, g_em[:, :], rt16[:, :], None, ALU.mult)
            te = v.tensor_scalar(ind_et[:, :], fm128[0:16, :], iop[0:16, :],
                                 None, ALU.is_equal)
            S["d_v"].inc(te, 1)

        @blk.gpsimd
        def _(g):
            g.wait_ge(S["a_tk"].h, 64)
            g.wait_ge(S["c_tg"].h, NT)
            g.iota(fm128[:, :].rearrange("p (a b) -> p a b", b=16),
                   [[0, 8], [1, 16]], channel_multiplier=0,
                   allow_small_or_imprecise_dtypes=True)
            g.iota(fd128[:, :].rearrange("p (a b) -> p a b", b=16),
                   [[1, 8], [0, 16]], channel_multiplier=0,
                   allow_small_or_imprecise_dtypes=True)
            g.iota(tmB[0:1, 0:128].rearrange("p (a b) -> p a b", b=16),
                   [[0, 8], [1, 16]], channel_multiplier=0,
                   allow_small_or_imprecise_dtypes=True)
            g.iota(tmB[0:1, 128:256].rearrange("p (a b) -> p a b", b=16),
                   [[1, 8], [0, 16]], channel_multiplier=0,
                   allow_small_or_imprecise_dtypes=True)
            it = g.iota(tokid[:, :], [[1, 512]], channel_multiplier=0,
                        allow_small_or_imprecise_dtypes=True)
            S["d_g"].inc(it, 1)

    for it_ in range(SINK):
        with nc.Block() as blk:
            t0 = S["d_t"].v
            v0 = S["d_v"].v
            s0 = S["d_s"].v

            @blk.tensor
            def _(t):
                t.wait_ge(S["d_v"].h, v0)
                for j in range(32):
                    tp = t.transpose(psTf[:, j * 16:(j + 1) * 16],
                                     T_em[0:16, j * 128:(j + 1) * 128],
                                     idt_f[0:16, 0:16])
                S["d_t"].inc(tp, 1)
                t.wait_ge(S["d_v"].h, v0 + 3)
                for rr in range(4):
                    ps = psA if rr % 2 == 0 else psB
                    if rr >= 2:
                        t.wait_ge(S["d_s"].h, s0 + rr + 1)
                    for jj in range(8):
                        j = rr * 8 + jj
                        tp = t.transpose(ps[0:16, jj * 128:(jj + 1) * 128],
                                         tmA[:, j * 16:(j + 1) * 16],
                                         idt_f[:, :])
                    S["d_t"].inc(tp, 1)

            @blk.vector
            def _(v):
                v.wait_ge(S["d_t"].h, t0 + 1)
                v.tensor_copy(tmB, psTf[:, 0:512])
                vgap(v)
                v3 = tmB.rearrange("p (a b) -> p a b", b=16)
                v.tensor_reduce(seg32[:, :], v3, AX.X, ALU.max)
                vgap(v)
                sb = v.tensor_tensor(
                    tmA.rearrange("p (a b) -> p a b", b=16), v3,
                    seg32[:, :, None].to_broadcast((128, 32, 16)), ALU.subtract)
                S["d_v"].inc(sb, 1)
                v.wait_ge(S["d_s"].h, s0 + 1)
                e3 = dmp_g.rearrange("p (a b) -> p a b", b=16)
                sm = v.tensor_reduce(seg32[:, :], e3, AX.X, ALU.add)
                S["d_v"].inc(sm, 1)
                v.wait_ge(S["d_s"].h, s0 + 2)
                sb2 = v.tensor_tensor(
                    tmA.rearrange("p (a b) -> p a b", b=16),
                    tmA.rearrange("p (a b) -> p a b", b=16),
                    seg32[:, :, None].to_broadcast((128, 32, 16)), ALU.subtract)
                S["d_v"].inc(sb2, 1)
                v.wait_ge(S["d_s"].h, s0 + 6)
                v.reduce_max(m16[:, :], T_em, AX.X)
                vgap(v)
                nm = v.tensor_scalar(nm16[:, :], m16[:, :], -1.0, None, ALU.mult)
                S["d_v"].inc(nm, 1)
                v.wait_ge(S["d_s"].h, s0 + 8)
                v.tensor_tensor(m16[:, :], m16[:, :], rs16[:, :], ALU.add)
                vgap(v)
                ts_ = v.tensor_scalar(T_em, T_em, m16[:, :], None, ALU.subtract)
                S["d_v"].inc(ts_, 1)

            @blk.scalar
            def _(s):
                s.wait_ge(S["d_v"].h, v0 + 1)
                ex = s.activation(dmp_g, tmA, AF.Exp)
                S["d_s"].inc(ex, 1)
                s.wait_ge(S["d_v"].h, v0 + 2)
                ln = s.activation(seg32[:, :], seg32[:, :], AF.Ln)
                S["d_s"].inc(ln, 1)
                for rr in range(4):
                    ps = psA if rr % 2 == 0 else psB
                    s.wait_ge(S["d_t"].h, t0 + 2 + rr)
                    ev = s.copy(T_em[0:16, rr * 1024:(rr + 1) * 1024],
                                ps[0:16, 0:1024])
                    S["d_s"].inc(ev, 1)
                s.wait_ge(S["d_v"].h, v0 + 4)
                ex2 = s.activation(A_em, T_em, AF.Exp, bias=nm16[:, :],
                                   accum_out=rs16[:, :])
                S["d_s"].inc(ex2, 1)
                sgap(s)
                ln2 = s.activation(rs16[:, :], rs16[:, :], AF.Ln)
                S["d_s"].inc(ln2, 1)

    with nc.Block() as blk:
        d7s = S["d_s"].v
        d7v = S["d_v"].v
        d7d = S["d_dma"].v

        @blk.scalar
        def _(s):
            s.wait_ge(S["d_v"].h, d7v)
            fa = s.activation(A_em, T_em, AF.Exp)
            S["d_s"].inc(fa, 1)

        @blk.sync
        def _(sy):
            sy.wait_ge(S["d_s"].h, d7s + 1)
            dm = sy.dma_start(adram[:, :], A_em[:, :])
            S["d_dma"].inc(dm, 16)
            sy.wait_ge(S["d_dma"].h, d7d + 16)
            dm = sy.dma_start(
                A_r, bass.AP(adram, 0, [[512, 8], [4096, 16], [1, 512]]))
            S["d_dma"].inc(dm, 16)

        @blk.vector
        def _(v):
            v.wait_ge(S["d_dma"].h, d7d + 32)
            v.tensor_copy(a_bf[:, :], A_r)
            tf = v.tensor_scalar(toff[:, :], pdiv[:, :], 512.0, None, ALU.mult)
            S["d_v"].inc(tf, 1)

        @blk.gpsimd
        def _(g):
            g.wait_ge(S["d_v"].h, d7v + 1)
            tk = g.tensor_scalar(tokid[:, :], tokid[:, :], toff[:, :], None,
                                 ALU.add)
            S["d_g"].inc(tk, 1)

    with nc.Block() as blk:
        b_t0 = S["d_t"].v
        b_v0 = S["d_v"].v
        b_g0 = S["d_g"].v

        @blk.vector
        def _(v):
            for k in range(BIS):
                if k > 0:
                    v.wait_ge(S["d_t"].h, b_t0 + 2 * k)
                    v.tensor_copy(cb_s[:, :], psTf[:, 0:1])
                    vgap(v)
                    v.tensor_scalar(msk_s[:, :], cb_s[:, :], float(CAP), None,
                                    ALU.is_ge)
                    v.tensor_scalar(nmsk_s[:, :], cb_s[:, :], float(CAP), None,
                                    ALU.is_lt)
                    vgap(v)
                    v.copy_predicated(lo_s[:, :], msk_s[:, :], mid_s[:, :])
                    v.copy_predicated(hi_s[:, :], nmsk_s[:, :], mid_s[:, :])
                    vgap(v)
                v.tensor_tensor(mid_s[:, :], lo_s[:, :], hi_s[:, :], ALU.add)
                vgap(v)
                v.tensor_scalar(mid_s[:, :], mid_s[:, :], 0.5, None, ALU.mult)
                vgap(v)
                ct = v.tensor_scalar(dmp_g, A_r, mid_s[:, :], 0.0, ALU.is_ge,
                                     ALU.add, accum_out=cnt_p[:, :])
                S["d_v"].inc(ct, 1)
                v.wait_ge(S["d_t"].h, b_t0 + 2 * k + 1)
                c1 = v.tensor_copy(c16[:, :], psTf[0:16, 4:5])
                S["d_v"].inc(c1, 1)
            v.wait_ge(S["d_t"].h, b_t0 + 2 * BIS)
            v.tensor_copy(cb_s[:, :], psTf[:, 0:1])
            vgap(v)
            v.tensor_scalar(msk_s[:, :], cb_s[:, :], float(CAP), None, ALU.is_ge)
            vgap(v)
            fu = v.copy_predicated(lo_s[:, :], msk_s[:, :], mid_s[:, :])
            S["d_v"].inc(fu, 1)

        @blk.tensor
        def _(t):
            for k in range(BIS):
                t.wait_ge(S["d_v"].h, b_v0 + 2 * k + 1)
                m1 = t.matmul(psTf[0:16, 4:5], ind_e[:, :], cnt_p[:, :],
                              start=True, stop=True)
                S["d_t"].inc(m1, 1)
                t.wait_ge(S["d_v"].h, b_v0 + 2 * k + 2)
                m2 = t.matmul(psTf[:, 0:1], ind_et[:, :], c16[:, :],
                              start=True, stop=True)
                S["d_t"].inc(m2, 1)

    with nc.Block() as blk:
        d9t = S["d_t"].v
        d9v = S["d_v"].v

        @blk.vector
        def _(v):
            vgap(v)
            v.tensor_scalar(sel_g, A_r, lo_s[:, :], None, ALU.is_ge)
            vgap(v)
            sc = v.tensor_tensor_scan(slots_g, sel_g, zer_g, 0.0, ALU.add,
                                      ALU.add)
            S["d_v"].inc(sc, 1)
            v.wait_ge(S["d_t"].h, d9t + 1)
            v.tensor_copy(cb_s[:, :], psTf[:, 0:1])
            vgap(v)
            v.tensor_scalar(slots_g, slots_g, cb_s[:, :], None, ALU.add)
            vgap(v)
            v.tensor_tensor(msf_g, slots_g, sel_g, ALU.mult)
            vgap(v)
            v.tensor_scalar(dmp_g, msf_g, float(CAP) + 0.5, None, ALU.is_le)
            vgap(v)
            v.tensor_tensor(msf_g, msf_g, dmp_g, ALU.mult)
            vgap(v)
            mi = v.tensor_scalar(msi[:, :], msf_g, 1.0, None, ALU.subtract)
            S["d_v"].inc(mi, 1)

        @blk.tensor
        def _(t):
            t.wait_ge(S["d_v"].h, d9v + 1)
            cm = t.matmul(psTf[:, 0:1], carry_m[:, :], slots_g[:, 511:512],
                          start=True, stop=True)
            S["d_t"].inc(cm, 1)

    with nc.Block() as blk:
        dAv = S["d_v"].v
        dAg = S["d_g"].v

        @blk.gpsimd
        def _(g):
            g.wait_ge(S["d_v"].h, dAv)
            l1 = g.local_scatter(dstA[:, :], tokid[:, :], msi[:, :], 128, CAP,
                                 512)
            S["d_g"].inc(l1, 1)
            l2 = g.local_scatter(dstB[:, :], a_bf[:, :], msi[:, :], 128, CAP,
                                 512)
            S["d_g"].inc(l2, 1)

    with nc.Block() as blk:
        dBt = S["d_t"].v
        dBv = S["d_v"].v
        dBg = S["d_g"].v

        @blk.vector
        def _(v):
            v.wait_ge(S["d_g"].h, dBg)
            v.tensor_copy(dstAf[:, :], dstA[:, :])
            cpy = v.tensor_copy(dstBf[:, :], dstB[:, :])
            S["d_v"].inc(cpy, 1)
            v.wait_ge(S["d_t"].h, dBt + 1)
            v.tensor_copy(idx_em[:, :], psTf[0:16, 0:256])
            cpy2 = v.tensor_copy(gat_em[:, :], psTf[0:16, 256:512])
            S["d_v"].inc(cpy2, 1)
            v.wait_ge(S["d_t"].h, dBt + 2)
            v.tensor_copy(idx_f[:, :], psA[:, 0:32])
            v.tensor_copy(gat_c[:, :], psA[:, 32:64])
            vgap(v)
            iv = v.tensor_copy(idx_i[:, :], idx_f[:, :])
            S["d_v"].inc(iv, 1)

        @blk.tensor
        def _(t):
            t.wait_ge(S["d_v"].h, dBv + 1)
            t.matmul(psTf[0:16, 0:256], ind_e[:, :], dstAf[:, :], start=True,
                     stop=True)
            mm = t.matmul(psTf[0:16, 256:512], ind_e[:, :], dstBf[:, :],
                          start=True, stop=True)
            S["d_t"].inc(mm, 1)
            t.wait_ge(S["d_v"].h, dBv + 2)
            for hh in range(2):
                t.transpose(psA[:, hh * 16:(hh + 1) * 16],
                            idx_em[0:16, hh * 128:(hh + 1) * 128],
                            idt_f[0:16, 0:16])
                tp = t.transpose(psA[:, 32 + hh * 16:32 + (hh + 1) * 16],
                                 gat_em[0:16, hh * 128:(hh + 1) * 128],
                                 idt_f[0:16, 0:16])
            S["d_t"].inc(tp, 1)

    dv_done = S["d_v"].v

    if dbg:
        with nc.Block() as blk:
            dgv = S["d_v"].v
            dgd = S["d_dma"].v

            @blk.sync
            def _(sy):
                sy.wait_ge(S["d_v"].h, dgv)
                S["d_dma"].inc(sy.dma_start(gdbg_h[:, :], g_em[:, :]), 16)
                S["d_dma"].inc(sy.dma_start(adbg_h[:, :], A_r), 16)
                S["d_dma"].inc(sy.dma_start(idbg_h[:, :], idx_f[:, :]), 16)
                S["d_dma"].inc(sy.dma_start(wdbg_h[:, :], gat_c[:, :]), 16)
                sy.wait_ge(S["d_dma"].h, dgd + 64)

    if upto < 3:
        ctx.close()
        nc.compile()
        return nc
    NEL = ne_lim
    with nc.Block() as blk:

        @blk.sync
        def _(sy):
            for e in range(NEL):
                if e >= 2:
                    sy.wait_ge(S["e_tei"].h, 2 * e - 2)
                bl = sy.dma_start(
                    b1raw[:, (e % 2) * 128:(e % 2 + 1) * 128],
                    b1_h[e:e + 1, :].rearrange("o (a p) -> (o a) p", p=128))
                S["e_b1"].inc(bl, 16)

        @blk.gpsimd
        def _(g):
            g.wait_ge(S["c_st"].h, 16 * 2 * NT)
            g.wait_ge(S["c_t3"].h, NT)
            g.wait_ge(S["a_z"].h, 16 * NT)
            g.wait_ge(S["d_v"].h, dv_done)
            for e in range(NEL):
                if e == 0:
                    g.wait_ge(S["c_ts"].h, 4 * NT)
                else:
                    g.wait_ge(S["e_th"].h, e)
                wl = g.dma_start(
                    a3[:, :, :],
                    w1_h[e, :, :].rearrange("(a p) f -> p a f", p=128))
                S["e_w1"].inc(wl, 16)
                if e >= 2:
                    g.wait_ge(S["e_teo"].h, 2 * (e - 1))
                b2l = g.dma_start(b2sl[e % 2], b2_h[e:e + 1, :])
                S["e_b2"].inc(b2l, 16)
                if e_stage >= 1:
                    if e >= 1:
                        g.wait_ge(S["e_tei"].h, 2 * e)
                    for hh in range(2):
                        ga = g.indirect_dma_start(
                            ei_sm[:, hh, :], None, ln3_h[:, :],
                            IndirectOffsetOnAxis(
                                idx_i[:, hh * 16 + e:hh * 16 + e + 1], 0))
                        S["e_gth"].inc(ga, 16)
                if e >= 1 and e_stage >= 5:
                    ep = e - 1
                    for hh in range(2):
                        g.wait_ge(S["e_veo"].h, 2 * ep + hh + 1)
                        g.wait_ge(S["e_sc"].h, 16 * (2 * ep + hh))
                        sc = g.indirect_dma_start(
                            acc_h[:, :],
                            IndirectOffsetOnAxis(
                                idx_i[:, hh * 16 + ep:hh * 16 + ep + 1], 0),
                            sb_f32[:, hh * 1024:(hh + 1) * 1024], None,
                            compute_op=ALU.add)
                        S["e_sc"].inc(sc, 16)
                if e == 0:
                    g.wait_ge(S["c_tpv"].h, 4 * NT)
                else:
                    g.wait_ge(S["e_teo"].h, 2 * e)
                w2l = g.dma_start(
                    b3[:, :, :],
                    w2_h[e, :, :].rearrange("(a p) d -> p a d", p=128))
                S["e_w2"].inc(w2l, 16)
            e = NEL - 1
            for hh in range(2):
                if e_stage < 5:
                    break
                g.wait_ge(S["e_veo"].h, 2 * e + hh + 1)
                g.wait_ge(S["e_sc"].h, 16 * (2 * e + hh))
                sc = g.indirect_dma_start(
                    acc_h[:, :],
                    IndirectOffsetOnAxis(
                        idx_i[:, hh * 16 + e:hh * 16 + e + 1], 0),
                    sb_f32[:, hh * 1024:(hh + 1) * 1024], None,
                    compute_op=ALU.add)
                S["e_sc"].inc(sc, 16)

        @blk.tensor
        def _(t):
            for e in range(NEL):
                if e_stage < 2:
                    break
                t.wait_ge(S["e_gth"].h, 32 * (e + 1))
                if e == 0:
                    t.wait_ge(S["c_vg"].h, NT)
                    t.wait_ge(S["c_se3"].h, NT)
                    t.wait_ge(S["d_v"].h, dv_done)
                for ch in range(2):
                    if ch == 0:
                        if e >= 1:
                            t.wait_ge(S["e_sei"].h, 2 * e)
                    else:
                        t.wait_ge(S["e_sei"].h, 2 * e + 1)
                    for dc in range(ND):
                        tp = t.transpose(
                            psTb[:, dc * 128:(dc + 1) * 128],
                            ei_sm[:, ch, dc * 128:(dc + 1) * 128], idt_b[:, :])
                    if ch == 1:
                        t.wait_ge(S["e_b1"].h, 16 * (e + 1))
                        tp = t.transpose(
                            psTf[:, 0:32],
                            b1raw[0:32, (e % 2) * 128:(e % 2 + 1) * 128],
                            idt_f[0:32, 0:32])
                    S["e_tei"].inc(tp, 1)
                if e_stage < 3:
                    continue
                t.wait_ge(S["e_sei"].h, 2 * (e + 1))
                t.wait_ge(S["e_w1"].h, 16 * (e + 1))
                for o in range(8):
                    psH = psA if o % 2 == 0 else psB
                    if o == 0 and e >= 1:
                        t.wait_ge(S["e_veo"].h, 2 * e)
                    if 8 * e + o >= 2:
                        t.wait_ge(S["e_sh"].h, 8 * e + o - 1)
                    for fl in range(4):
                        fc = o * 4 + fl
                        dst = psH[:, fl * 256:(fl + 1) * 256]
                        for dc in range(ND):
                            mm = t.matmul(dst,
                                          a3[:, dc, fc * 128:(fc + 1) * 128],
                                          stgE[:, dc, :], start=(dc == 0),
                                          stop=(dc == 7))
                    S["e_thq"].inc(mm, 1)
                S["e_th"].inc(t.nop(), 1)
                if e_stage < 4:
                    continue
                t.wait_ge(S["e_sh"].h, 8 * (e + 1))
                t.wait_ge(S["e_w2"].h, 16 * (e + 1))
                t.wait_ge(S["e_b2"].h, 16 * (e + 1))
                for ch in range(2):
                    psE = psPV if ch == 0 else psA
                    if e >= 1:
                        t.wait_ge(S["e_veo"].h, 2 * e - 1 + ch)
                    for fc in range(32):
                        lt_ = sbigE[:, fc, ch * 128:(ch + 1) * 128]
                        t.matmul(psE[:, 0:512], lt_, b3[:, fc, 0:512],
                                 start=(fc == 0), stop=False)
                        t.matmul(psE[:, 512:1024], lt_, b3[:, fc, 512:1024],
                                 start=(fc == 0), stop=False)
                    t.matmul(psE[:, 0:512], onesl[e % 2][:, :],
                             b2sl[e % 2][:, 0:512], start=False, stop=True)
                    mm = t.matmul(psE[:, 512:1024], onesl[e % 2][:, :],
                                  b2sl[e % 2][:, 512:1024], start=False,
                                  stop=True)
                    S["e_teo"].inc(mm, 1)

        @blk.scalar
        def _(s):
            s.wait_ge(S["c_tg"].h, NT)
            for e in range(NEL):
                if e_stage < 2:
                    break
                for ch in range(2):
                    s.wait_ge(S["e_tei"].h, 2 * e + ch + 1)
                    ee = s.copy(
                        stgE[:, 0:8, ch * 128:(ch + 1) * 128],
                        psTb[:, 0:1024].rearrange("p (a b) -> p a b", b=128))
                    if ch == 1:
                        ee = s.copy(b1c[:, e % 2, :], psTf[:, 0:32])
                        sgap(s)
                    S["e_sei"].inc(ee, 1)
                for o in range(8):
                    if e_stage < 3:
                        break
                    psH = psA if o % 2 == 0 else psB
                    s.wait_ge(S["e_thq"].h, 8 * e + o + 1)
                    for fl in range(4):
                        fc = o * 4 + fl
                        srcp = psH[:, fl * 256:(fl + 1) * 256]
                        ge = s.activation(sbigE[:, fc, :], srcp,
                                          AF.Gelu_apprx_tanh,
                                          bias=b1c[:, e % 2, fc:fc + 1])
                    S["e_sh"].inc(ge, 1)

        @blk.vector
        def _(v):
            for e in range(NEL):
                if e_stage < 4:
                    break
                for ch in range(2):
                    psE = psPV if ch == 0 else psA
                    v.wait_ge(S["e_teo"].h, 2 * e + ch + 1)
                    if e >= 1:
                        v.wait_ge(S["e_sc"].h, 16 * (2 * (e - 1) + ch + 1))
                    gcol = gat_c[:, ch * 16 + e:ch * 16 + e + 1]
                    v.tensor_scalar(sb_f32[:, ch * 1024:ch * 1024 + 512],
                                    psE[:, 0:512], gcol, None, ALU.mult)
                    ve = v.tensor_scalar(
                        sb_f32[:, ch * 1024 + 512:(ch + 1) * 1024],
                        psE[:, 512:1024], gcol, None, ALU.mult)
                    S["e_veo"].inc(ve, 1)

    if dbg:
        with nc.Block() as blk:
            edd = S["d_dma"].v

            @blk.sync
            def _(sy):
                sy.wait_ge(S["e_sc"].h, S["e_sc"].v)
                S["d_dma"].inc(sy.dma_start(eidbg_h[:, :], stg[:, 1024:3072]), 16)
                S["d_dma"].inc(sy.dma_start(efdbg_h[:, :], eif_t[:, :]), 16)
                S["d_dma"].inc(sy.dma_start(hdbg_h[:, :], sbig[:, :]), 16)
                S["d_dma"].inc(sy.dma_start(eodbg_h[:, :], sb_f32[:, 0:2048]), 16)
                S["d_dma"].inc(sy.dma_start(w2dbg_h[:, :], ar_b[:, :]), 16)
                S["d_dma"].inc(sy.dma_start(b2dbg_h[:, :], wq_s[32:65, 0:1024]), 16)
                sy.wait_ge(S["d_dma"].h, edd + 96)

    if upto < 4:
        ctx.close()
        nc.compile()
        return nc
    with nc.Block() as blk:

        @blk.sync
        def _(sy):
            aba_h = [sb_f32[:, 0:1024], em[:, 512:1536]]
            oba_h = [sb_f32[:, 1024:2048], em[:, 1536:2560]]
            sy.wait_ge(S["e_sc"].h, S["e_sc"].v)
            for i in range(NT):
                if i >= 2:
                    sy.wait_ge(S["f_v"].h, i - 1)
                S["f_in"].inc(sy.dma_start(
                    aba_h[i % 2], acc_h[i * 128:(i + 1) * 128, :]), 16)
                S["f_in"].inc(sy.dma_start(
                    stg[:, (i % 2) * 1024:(i % 2 + 1) * 1024],
                    x2_h[i * 128:(i + 1) * 128, :]), 16)
                sy.wait_ge(S["f_v"].h, i + 1)
                S["f_out"].inc(sy.dma_start(
                    out_h[i * 128:(i + 1) * 128, :], oba_h[i % 2]), 16)
            sy.wait_ge(S["f_out"].h, 16 * NT)

        @blk.vector
        def _(v):
            aba_h = [sb_f32[:, 0:1024], em[:, 512:1536]]
            oba_h = [sb_f32[:, 1024:2048], em[:, 1536:2560]]
            for i in range(NT):
                ab = aba_h[i % 2]
                ob = oba_h[i % 2]
                xb = stg[:, (i % 2) * 1024:(i % 2 + 1) * 1024]
                v.wait_ge(S["f_in"].h, 32 * (i + 1))
                if i >= 2:
                    v.wait_ge(S["f_out"].h, 16 * (i - 1))
                S["f_vs"].inc(bn_var(v, ab), 1)
                v.wait_ge(S["f_ss"].h, i + 1)
                v.reciprocal(mst[:, 5:6], mst[:, 4:5])
                vgap(v)
                v.tensor_scalar(ob, ab, mst[:, 1:2], mst[:, 5:6],
                                ALU.subtract, ALU.mult)
                vgap(v)
                fo = v.tensor_tensor(ob, ob, xb, ALU.add)
                S["f_v"].inc(fo, 1)

        @blk.scalar
        def _(s):
            for i in range(NT):
                s.wait_ge(S["f_vs"].h, i + 1)
                st = s.activation(mst[:, 4:5], mst[:, 2:3], AF.Sqrt, bias=epsb[:, :])
                S["f_ss"].inc(st, 1)

    ctx.close()
    nc.compile()
    return nc


def kernel(L0_attn_X, L0_attn_W_Q, L0_attn_W_K, L0_attn_W_V, L0_ffn_gate_query,
           L0_ffn_log_temp, L0_ffn_w1, L0_ffn_b1, L0_ffn_w2, L0_ffn_b2):
    nc = build_nc()
    base = {
        "wq": np.ascontiguousarray(L0_attn_W_Q, dtype=np.float32),
        "wk": np.ascontiguousarray(L0_attn_W_K, dtype=np.float32),
        "wv": np.ascontiguousarray(L0_attn_W_V, dtype=np.float32),
        "gq": np.ascontiguousarray(L0_ffn_gate_query, dtype=np.float32),
        "ltemp": np.asarray(L0_ffn_log_temp, dtype=np.float32).reshape(1, 1),
        "w1": np.ascontiguousarray(L0_ffn_w1, dtype=np.float32),
        "b1": np.ascontiguousarray(L0_ffn_b1, dtype=np.float32),
        "w2": np.ascontiguousarray(L0_ffn_w2, dtype=np.float32),
        "b2": np.ascontiguousarray(L0_ffn_b2, dtype=np.float32),
    }
    in_maps = []
    for b in range(8):
        m = dict(base)
        m["x"] = np.ascontiguousarray(L0_attn_X[b], dtype=np.float32)
        in_maps.append(m)
    res = run_bass_kernel_spmd(nc, in_maps, core_ids=list(range(8)))
    return np.stack([r["out"] for r in res.results], axis=0)
